# version 16
# speedup vs baseline: 1.0693x; 1.0036x over previous
import math
from contextlib import ExitStack

import numpy as np
import ml_dtypes

import concourse.bass as bass
import concourse.mybir as mybir
from concourse.bass_utils import run_bass_kernel_spmd

F32 = mybir.dt.float32
BF16 = mybir.dt.bfloat16
I32 = mybir.dt.int32
I16 = mybir.dt.int16
U32 = mybir.dt.uint32
AF = mybir.ActivationFunctionType
ALU = mybir.AluOpType
AX = mybir.AxisListType
BFNP = ml_dtypes.bfloat16

L = 8192
D = 1024
H = 4
DK = 256
DV = 256
SEG = 1024
NSEG = L // SEG
TT = 512
BLK = 128
NBLK = L // BLK


class Prog:

    def __init__(self, nc, sems):
        self.nc = nc
        self.sems = sems
        self.active = None
        self.handle = None
        self.counts = {k: 0 for k in sems}
        self.dma_rr = {}

    def begin(self, active, handle):
        self.active = active
        self.handle = handle
        self.counts = {k: 0 for k in self.sems}
        self.dma_rr = {}

    def on(self, engine):
        return self.active == engine

    def mark(self, sem):
        return self.counts[sem]

    def do(self, engine, fn, sem=None, n=1, drain=None):
        if drain is None:
            drain = engine in ("vector", "scalar")
        if sem is not None:
            self.counts[sem] += n
        if self.active == engine:
            inst = fn(self.handle)
            if sem is not None and inst is not None:
                inst.then_inc(self.sems[sem], n)
            if drain:
                self.handle.drain()

    DMA_POOL = 6

    def dma(self, engine, out, in_, sem=None, transpose=False, fn=None):
        q = self.dma_rr.setdefault(engine, 0)
        slot = f"dq_{engine}_{q % self.DMA_POOL}"
        self.dma_rr[engine] = q + 1
        prev = self.counts[slot]
        if prev > 0 and self.active == engine:
            self.handle.wait_ge(self.sems[slot], prev)
        self.counts[slot] += 16
        if self.active == engine:
            if fn is not None:
                inst = fn(self.handle)
            elif transpose:
                inst = self.handle.dma_start(out=out, in_=in_, transpose=True)
            else:
                inst = self.handle.dma_start(out=out, in_=in_)
            inst.then_inc(self.sems[slot], 16)
        return (slot, self.counts[slot])

    def wait_tok(self, engine, tok):
        if tok is None:
            return
        s, v = tok
        if self.active == engine:
            self.handle.wait_ge(self.sems[s], v)

    def wait(self, engine, sem, val=None):
        v = self.counts[sem] if val is None else val
        if v <= 0:
            return
        if self.active == engine:
            self.handle.wait_ge(self.sems[sem], v)

    def barrier_from(self, engine, sem_list):
        for s in sem_list:
            self.wait(engine, s)

    def final_wait(self, engine):
        if self.active == engine:
            for name, s in self.sems.items():
                if self.counts[name] > 0:
                    self.handle.wait_ge(s, self.counts[name])


def run_program(nc, build, sem_names):
    all_names = list(sem_names)
    for eng in ("sync", "gpsimd", "scalar", "vector", "tensor"):
        for i in range(Prog.DMA_POOL):
            all_names.append(f"dq_{eng}_{i}")
    with ExitStack() as stack:
        sems = {n: stack.enter_context(nc.semaphore(n)) for n in all_names}
        with nc.Block() as block:
            p = Prog(nc, sems)

            @block.sync
            def _(eng):
                p.begin("sync", eng)
                build(p)

            @block.tensor
            def _(eng):
                p.begin("tensor", eng)
                build(p)

            @block.vector
            def _(eng):
                p.begin("vector", eng)
                build(p)

            @block.scalar
            def _(eng):
                p.begin("scalar", eng)
                build(p)

            @block.gpsimd
            def _(eng):
                p.begin("gpsimd", eng)
                build(p)


L1_SEMS = ["ld", "st", "tr", "mm", "v", "a", "g"]


def emit_frontend(p, t, sb):
    p.do("gpsimd", lambda e: e.memset(sb["ones_col"][:], 1.0), "g")
    p.do("gpsimd", lambda e: e.memset(sb["ones_row"][:], 1.0), "g")
    p.do("gpsimd", lambda e: e.memset(sb["ones_row_f"][:], 1.0), "g")
    p.do("gpsimd", lambda e: e.memset(sb["c_eps_ln"][:], 1e-5), "g")
    p.do("gpsimd", lambda e: e.memset(sb["c_eps_l2"][:], 1e-6), "g")
    for ob in range(6):
        p.do("gpsimd", lambda e, ob=ob: e.memset(sb["pre"][ob][:, 0:3], 0.0), "g")

    for eng in ("scalar", "vector", "tensor"):
        p.wait(eng, "g")

    tok_w = []
    for fb in range(8):
        tok_w.append(p.dma("gpsimd", sb["w_fm"][fb][:], t["wqkv_fm"][fb * 128:(fb + 1) * 128, :]))
    tok_cw = p.dma("gpsimd", sb["convw"][:], t["convw_r"])

    apply_marks = {}
    st_toks = {}
    bst_tok = None
    bst_all = []
    tiles_per_seg = SEG // 128
    for s in range(NSEG):
        for i in range(tiles_per_seg):
            ig = s * tiles_per_seg + i
            buf = ig % 2
            if ig >= 2:
                p.wait("sync", "a", apply_marks[ig - 2])
            tok_hs = p.dma("sync", sb["hs_t"][buf][:], t["hs"][ig * 128:(ig + 1) * 128, :])
            p.wait_tok("vector", tok_hs)
            p.do("vector", lambda e, buf=buf: e.reduce_sum(sb["ln_m"][buf][:], sb["hs_t"][buf][:], axis=AX.X), "v")
            p.wait("scalar", "v")
            p.wait_tok("scalar", tok_hs)
            p.do("scalar", lambda e, buf=buf: e.activation(sb["ln_scr"][:], sb["hs_t"][buf][:], AF.Square, accum_out=sb["ln_sq"][buf][:]), "a")
            p.wait("vector", "a")
            p.do("vector", lambda e, buf=buf: e.tensor_scalar_mul(sb["ln_m"][buf][:], sb["ln_m"][buf][:], 1.0 / 1024.0))
            p.do("vector", lambda e, buf=buf: e.tensor_tensor(sb["ln_m2"][buf][:], sb["ln_m"][buf][:], sb["ln_m"][buf][:], ALU.mult))
            p.do("vector", lambda e, buf=buf: e.scalar_tensor_tensor(sb["ln_v"][buf][:], sb["ln_sq"][buf][:], 1.0 / 1024.0, sb["ln_m2"][buf][:], ALU.mult, ALU.subtract), "v")
            p.wait("scalar", "v")
            p.do("scalar", lambda e, buf=buf: e.activation(sb["ln_sd"][buf][:], sb["ln_v"][buf][:], AF.Sqrt, bias=sb["c_eps_ln"][:]), "a")
            p.wait("vector", "a")
            p.do("vector", lambda e, buf=buf: e.reciprocal(sb["ln_r"][buf][:], sb["ln_sd"][buf][:]))
            p.do("vector", lambda e, buf=buf: e.scalar_tensor_tensor(sb["ln_b"][buf][:], sb["ln_m"][buf][:], -1.0, sb["ln_r"][buf][:], ALU.mult, ALU.mult), "v")
            p.wait("scalar", "v")
            if ig >= 2:
                p.wait_tok("scalar", st_toks[ig - 2])
            p.do("scalar", lambda e, buf=buf: e.activation(sb["xln_t"][buf][:], sb["hs_t"][buf][:], AF.Identity, bias=sb["ln_b"][buf][:], scale=sb["ln_r"][buf][:]), "a")
            apply_marks[ig] = p.mark("a")
            p.wait("gpsimd", "a")
            st_toks[ig] = p.dma("gpsimd", t["x_ln_hbm"][ig * 128:(ig + 1) * 128, :], sb["xln_t"][buf][:])

        for i in range(tiles_per_seg):
            p.wait_tok("sync", st_toks[s * tiles_per_seg + i])
        if s > 0:
            p.wait("sync", "mm", p.counts["mm"])
        tok_tr = []
        for fb in range(8):
            tok_tr.append(p.dma("sync", sb["x_fm"][fb][:],
                                t["x_ln_hbm"][s * SEG:(s + 1) * SEG, fb * 128:(fb + 1) * 128],
                                transpose=True))

        if s > 0:
            p.wait("vector", "a")
            for ob in range(6):
                p.do("vector", lambda e, ob=ob: e.tensor_copy(sb["pre"][ob][:, 0:3], sb["pre"][ob][:, SEG:SEG + 3]), "v")

        for tk in tok_tr:
            p.wait_tok("tensor", tk)
        for tk in tok_w:
            p.wait_tok("tensor", tk)
        if s > 0:
            p.wait("tensor", "v")
        for tt in range(SEG // TT):
            tok0 = tt * TT
            for ob in range(6):
                bank = sb["pp"][(tt * 6 + ob) % 2]
                for fb in range(8):
                    p.do("tensor", lambda e, fb=fb, ob=ob, bank=bank, tok0=tok0: e.matmul(
                        bank[:], sb["w_fm"][fb][:, ob * 128:(ob + 1) * 128],
                        sb["x_fm"][fb][:, tok0:tok0 + TT],
                        start=(fb == 0), stop=(fb == 7)),
                        sem="mm" if fb == 7 else None)
                p.wait("vector", "mm")
                p.do("vector", lambda e, ob=ob, bank=bank, tok0=tok0: e.tensor_copy(
                    sb["pre"][ob][:, 3 + tok0:3 + tok0 + TT], bank[:]), "v")
                if tt * 6 + ob >= 1:
                    p.wait("tensor", "v")
            for fb in range(8):
                p.do("tensor", lambda e, fb=fb, tok0=tok0: e.matmul(
                    sb["pb"][:], sb["w_fm"][fb][:, 768:769],
                    sb["x_fm"][fb][:, tok0:tok0 + TT],
                    start=(fb == 0), stop=(fb == 7)),
                    sem="mm" if fb == 7 else None)
            p.wait("scalar", "mm")
            if bst_tok is not None:
                p.wait_tok("scalar", bst_tok)
            p.do("scalar", lambda e, tok0=tok0: e.activation(
                sb["beta_seg"][0:1, tok0:tok0 + TT], sb["pb"][:], AF.Sigmoid), "a")
            p.wait("tensor", "a")

        p.wait("vector", "v")
        p.wait_tok("vector", tok_cw)
        for ob in range(6):
            src = sb["pre"][ob]
            dseg = sb["qkv"][ob][:, s * SEG:(s + 1) * SEG]
            wv = sb["convw"]
            p.do("vector", lambda e, src=src, dseg=dseg, wv=wv, ob=ob: e.tensor_scalar(
                dseg, src[:, 0:SEG], wv[:, ob * 4 + 0:ob * 4 + 1], None, ALU.mult))
            for j in (1, 2, 3):
                p.do("vector", lambda e, src=src, dseg=dseg, wv=wv, ob=ob, j=j: e.scalar_tensor_tensor(
                    dseg, src[:, j:j + SEG], wv[:, ob * 4 + j:ob * 4 + j + 1],
                    dseg, ALU.mult, ALU.add),
                    sem="v" if j == 3 else None)
            p.wait("scalar", "v")
            p.do("scalar", lambda e, dseg=dseg: e.activation(dseg, dseg, AF.Silu), "a")

        p.wait("scalar", "a")
        for qk in range(2):
            for half in range(2):
                seg_src = sb["qkv"][qk * 2 + half][:, s * SEG:(s + 1) * SEG]
                p.do("scalar", lambda e, seg_src=seg_src, half=half: e.activation(
                    sb["sq_scr"][half][:], seg_src, AF.Square), "a")
            p.wait("tensor", "a")
            for nt in range(SEG // 512):
                for half in range(2):
                    p.do("tensor", lambda e, nt=nt, half=half: e.matmul(
                        sb["pn"][:], sb["ones_col"][:],
                        sb["sq_scr"][half][:, nt * 512:(nt + 1) * 512],
                        start=(half == 0), stop=(half == 1)),
                        sem="mm" if half == 1 else None)
                p.wait("vector", "mm")
                p.do("vector", lambda e, nt=nt: e.tensor_copy(
                    sb["ss_row"][0:1, nt * 512:(nt + 1) * 512], sb["pn"][:]), "v")
                p.wait("tensor", "v")
            for nt in range(SEG // 512):
                p.do("tensor", lambda e, nt=nt: e.matmul(
                    sb["pc"][:], sb["ones_row_f"][:],
                    sb["ss_row"][0:1, nt * 512:(nt + 1) * 512], start=True, stop=True),
                    sem="mm")
                p.wait("scalar", "mm")
                p.do("scalar", lambda e, nt=nt: e.activation(
                    sb["ssb"][:, nt * 512:(nt + 1) * 512], sb["pc"][:], AF.Sqrt, bias=sb["c_eps_l2"][:]), "a")
                p.wait("tensor", "a")
            p.wait("vector", "a")
            p.do("vector", lambda e: e.reciprocal(sb["rs"][:], sb["ssb"][:]), "v")
            for half in range(2):
                seg_src = sb["qkv"][qk * 2 + half][:, s * SEG:(s + 1) * SEG]
                p.do("vector", lambda e, seg_src=seg_src: e.tensor_tensor(
                    seg_src, seg_src, sb["rs"][:], ALU.mult), "v")
            p.wait("scalar", "v")

        p.wait("tensor", "a")
        for nt in range(SEG // 512):
            p.do("tensor", lambda e, nt=nt: e.matmul(
                sb["pc"][:], sb["ones_row"][:], sb["beta_seg"][0:1, nt * 512:(nt + 1) * 512],
                start=True, stop=True), sem="mm")
            p.wait("vector", "mm")
            p.do("vector", lambda e, nt=nt, col0=s * SEG + nt * 512: e.tensor_copy(
                sb["beta_bc"][:, col0:col0 + 512], sb["pc"][:]), "v")
            p.wait("tensor", "v")
        p.wait("gpsimd", "a")
        bst_tok = p.dma("gpsimd", t["beta_hbm"][s * SEG:(s + 1) * SEG], sb["beta_seg"][0:1, :])
        bst_all.append(bst_tok)
    p.fe_bst_toks = bst_all


def emit_frontend_tail(p, t, sb):
    for tk in p.fe_bst_toks:
        p.wait_tok("sync", tk)
    tok = p.dma("sync", sb["beta_tok"][:],
                t["beta_hbm"].rearrange("(c p) -> c p", p=128), transpose=True)
    p.wait_tok("vector", tok)
    p.do("vector", lambda e: e.tensor_copy(sb["beta_tok_f"][:], sb["beta_tok"][:]), "v")


def psum_pool(nc, stack):
    ent = lambda cm: stack.enter_context(cm)
    ps = {}
    for i in range(3):
        ps[f"PS{i}"] = ent(nc.psum_tensor(f"PS{i}", [128, 512], F32))
    ps["PS3a"] = ent(nc.psum_tensor("PS3a", [128, 128], BF16))
    ps["PS3b"] = ent(nc.psum_tensor("PS3b", [128, 256], F32))
    ps["PS4"] = ent(nc.psum_tensor("PS4", [128, 256], F32))
    ps["PS5"] = ent(nc.psum_tensor("PS5", [128, 256], F32))
    ps["PS6"] = ent(nc.psum_tensor("PS6", [64, 512], F32))
    return ps


def frontend_sbuf(nc, stack, ps):
    ent = lambda cm: stack.enter_context(cm)
    sb = {}
    sb["w_fm"] = [ent(nc.sbuf_tensor(f"wfm{i}", [128, 769], BF16)) for i in range(8)]
    sb["convw"] = ent(nc.sbuf_tensor("convw", [128, 24], F32))
    sb["hs_t"] = [ent(nc.sbuf_tensor(f"hst{i}", [128, 1024], F32)) for i in range(2)]
    sb["xln_t"] = [ent(nc.sbuf_tensor(f"xlnt{i}", [128, 1024], BF16)) for i in range(2)]
    for nm in ["ln_m", "ln_sq", "ln_m2", "ln_v", "ln_sd", "ln_r", "ln_b"]:
        sb[nm] = [ent(nc.sbuf_tensor(nm + str(i), [128, 1], F32)) for i in range(2)]
    sb["ln_scr"] = ent(nc.sbuf_tensor("ln_scr", [128, 1024], F32))
    sb["x_fm"] = [ent(nc.sbuf_tensor(f"xfm{i}", [128, SEG], BF16)) for i in range(8)]
    sb["pre"] = [ent(nc.sbuf_tensor(f"pre{i}", [128, SEG + 3], BF16)) for i in range(6)]
    sb["qkv"] = [ent(nc.sbuf_tensor(f"qkv{i}", [128, L], BF16)) for i in range(6)]
    sb["beta_seg"] = ent(nc.sbuf_tensor("beta_seg", [1, SEG], BF16))
    sb["beta_tok"] = ent(nc.sbuf_tensor("beta_tok", [128, L // 128], BF16))
    sb["beta_tok_f"] = ent(nc.sbuf_tensor("beta_tok_f", [128, L // 128], F32))
    sb["beta_bc"] = ent(nc.sbuf_tensor("beta_bc", [128, L], BF16))
    sb["sq_scr"] = [ent(nc.sbuf_tensor(f"sqscr{i}", [128, SEG], BF16)) for i in range(2)]
    sb["ss_row"] = ent(nc.sbuf_tensor("ss_row", [1, SEG], F32))
    sb["ssb"] = ent(nc.sbuf_tensor("ssb", [128, SEG], F32))
    sb["rs"] = ent(nc.sbuf_tensor("rs", [128, SEG], F32))
    sb["ones_col"] = ent(nc.sbuf_tensor("ones_col", [128, 1], BF16))
    sb["ones_row"] = ent(nc.sbuf_tensor("ones_row", [1, 128], BF16))
    sb["ones_row_f"] = ent(nc.sbuf_tensor("ones_row_f", [1, 128], F32))
    sb["c_eps_ln"] = ent(nc.sbuf_tensor("c_eps_ln", [128, 1], F32))
    sb["c_eps_l2"] = ent(nc.sbuf_tensor("c_eps_l2", [128, 1], F32))
    sb["pp"] = [ps["PS0"], ps["PS1"]]
    sb["pb"] = ps["PS6"][0:1, :]
    sb["pn"] = ps["PS6"][32:33, :]
    sb["pc"] = ps["PS2"]
    return sb


def stage_frontend_inputs(hs_b, qw, kw, vw, bw, cq, ck, cv, h):
    sl = slice(h * 256, (h + 1) * 256)
    wq = qw[sl].T
    wk = kw[sl].T
    wv = vw[sl].T
    wb = bw[h:h + 1].T
    wqkv = np.concatenate([wq, wk, wv, wb], axis=1).astype(np.float32)
    convw = np.concatenate([cq[sl], ck[sl], cv[sl]], axis=0).astype(np.float32)
    convw_r = convw.reshape(6, 128, 4).transpose(1, 0, 2).reshape(128, 24).copy()
    return {"hs": np.ascontiguousarray(hs_b, np.float32),
            "wqkv_fm": wqkv, "convw_r": convw_r}


def scan_sbuf(nc, stack, ps):
    ent = lambda cm: stack.enter_context(cm)
    sb = {}
    for nm in ["mA", "mAT", "mLt", "mGt"]:
        sb[nm] = ent(nc.sbuf_tensor("sc_" + nm, [128, 128], F32))
    sb["I_bf"] = ent(nc.sbuf_tensor("sc_I_bf", [128, 128], BF16))
    sb["S0"] = [ent(nc.sbuf_tensor(f"S0_{i}", [128, 256], BF16)) for i in range(2)]
    sb["Sf"] = [ent(nc.sbuf_tensor(f"Sf_{i}", [128, 256], F32)) for i in range(2)]
    for nm in ["kbt0", "kbt1", "A", "AT", "F1T", "A2", "A2T", "A4", "A4T",
               "A8", "A8T", "A16", "P1", "P2", "MT", "Lt", "Gt", "Wdk0", "Wdk1"]:
        sb[nm] = ent(nc.sbuf_tensor(nm, [128, 128], BF16))
    for nm in ["Ktok", "Vtok", "KBtok", "Utok", "u_adj", "dtok"]:
        sb[nm] = ent(nc.sbuf_tensor(nm, [128, 256], BF16))
    sb["ps_S"] = [ps["PS4"], ps["PS5"]]
    sb["ps_u"] = ps["PS0"][:, 0:256]
    sb["ps_o"] = ps["PS1"][:, 0:256]
    sb["ps_g"] = ps["PS2"][:, 0:128]
    sb["ps_t"] = ps["PS3a"]
    sb["ps_x"] = ps["PS3b"]
    return sb


def emit_scan(p, t, fe, sc, fs=None, nblk=NBLK):
    q0, q1, k0, k1 = fe["qkv"][0], fe["qkv"][1], fe["qkv"][2], fe["qkv"][3]
    v0, v1 = fe["qkv"][4], fe["qkv"][5]

    mtoks = [p.dma("sync", sc[nm][:], t[nm]) for nm in ["mA", "mAT", "mLt", "mGt", "I_bf"]]
    p.do("gpsimd", lambda e: e.memset(sc["S0"][0][:], 0.0), "g")
    p.do("gpsimd", lambda e: e.memset(sc["S0"][1][:], 0.0), "g")
    p.do("gpsimd", lambda e: e.memset(sc["Sf"][0][:], 0.0), "g")
    p.do("gpsimd", lambda e: e.memset(sc["Sf"][1][:], 0.0), "g")
    for tk in mtoks:
        p.wait_tok("tensor", tk)
        p.wait_tok("vector", tk)
    p.wait("tensor", "g")
    p.wait("vector", "g")
    p.wait("scalar", "g")

    ENG = {"v": "vector", "a": "scalar", "mm": "tensor"}

    def mmg(bank, ops, deps, sem="mm"):
        for s, mk in deps:
            if s is not None:
                p.wait("tensor", s, mk)
        n = len(ops)
        for j, (lh, rh) in enumerate(ops):
            p.do("tensor", lambda e, lh=lh, rh=rh, j=j, n=n, bank=bank: e.matmul(
                bank, lh, rh, start=(j == 0), stop=(j == n - 1)),
                sem=sem if j == n - 1 else None)
        return ("mm", p.mark(sem))

    def EV(dst, bank, gmark, mask=None, pre_waits=()):
        if mask is None:
            for s, mk in pre_waits:
                p.wait("scalar", s, mk)
            p.wait("scalar", gmark[0], gmark[1])
            p.do("scalar", lambda e: e.copy(dst, bank), "a")
            return ("a", p.mark("a"))
        for s, mk in pre_waits:
            p.wait("vector", s, mk)
        p.wait("vector", gmark[0], gmark[1])
        p.do("vector", lambda e: e.tensor_tensor(dst, bank, mask, ALU.mult), "v")
        return ("v", p.mark("v"))

    wfm0, wfm1 = fe["w_fm"][0], fe["w_fm"][1]
    hb = [
        {"Wdk0": sc["Wdk0"][:], "Wdk1": sc["Wdk1"][:], "Utok": sc["Utok"][:],
         "Ktok": sc["Ktok"][:], "Lt": sc["Lt"][:], "Gt": sc["Gt"][:]},
        {"Wdk0": wfm0[:, 0:128], "Wdk1": wfm0[:, 128:256], "Utok": wfm1[:, 0:256],
         "Ktok": wfm1[:, 256:512], "Lt": wfm0[:, 256:384], "Gt": wfm0[:, 384:512]},
    ]
    dst_tok_box = [None]
    vtok_sp_box = [None]
    H = [dict(), dict()]
    core_done = [("v", 0)]
    s0_mark = [("a", 0)]
    pg_free = [("v", 0)]
    px_free = [("a", 0)]
    o_evac = [("a", 0)]
    prev_ATg = [("mm", 0)]

    def prep_chunks(blk):
        c = slice(blk * 128, (blk + 1) * 128)
        B = hb[blk % 2]
        M = H[blk % 2]
        pg = sc["ps_g"][:]

        def p1():
            p.wait("vector", *prev_ATg[0])
            p.do("vector", lambda e: e.tensor_tensor(sc["kbt0"][:], k0[:, c], fe["beta_bc"][:, c], ALU.mult))
            p.do("vector", lambda e: e.tensor_tensor(sc["kbt1"][:], k1[:, c], fe["beta_bc"][:, c], ALU.mult), "v")
            kbm = ("v", p.mark("v"))
            g = mmg(pg, [(sc["kbt0"][:], k0[:, c]), (sc["kbt1"][:], k1[:, c])], [kbm, pg_free[0]])
            M["A"] = EV(sc["A"][:], pg, g, sc["mA"][:])
            g = mmg(pg, [(k0[:, c], sc["kbt0"][:]), (k1[:, c], sc["kbt1"][:])], [M["A"]])
            prev_ATg[0] = g
            M["AT"] = EV(sc["AT"][:], pg, g, sc["mAT"][:])
            pg_free[0] = M["AT"]
            p.wait("vector", *M["AT"])
            p.do("vector", lambda e: e.tensor_tensor(sc["F1T"][:], sc["I_bf"][:], sc["AT"][:], ALU.subtract), "v")
            M["F1T"] = ("v", p.mark("v"))

        def sq(pairs):
            def f():
                for lev in pairs:
                    for which, (dst, lh, rh) in enumerate(lev):
                        if which == 0:
                            g = mmg(pg, [(sc[lh][:], sc[rh][:])],
                                    [M.get(lh, M["AT"]), M.get(rh, M["A"]), pg_free[0]])
                            M[dst] = EV(sc[dst][:], pg, g)
                            pg_free[0] = M[dst]
                        else:
                            px = sc["ps_x"][:, 0:128]
                            g = mmg(px, [(sc[lh][:], sc[rh][:])],
                                    [M.get(lh, M["AT"]), M.get(rh, M["A"]), px_free[0]])
                            p.wait("vector", *g)
                            p.do("vector", lambda e, dst=dst, px=px: e.tensor_copy(sc[dst][:], px), "v")
                            M[dst] = ("v", p.mark("v"))
                            px_free[0] = M[dst]
            return f

        def qgrp(dst, a, b, aT):
            def f():
                px = sc["ps_x"][:, 0:128]
                g = mmg(px, [(sc["I_bf"][:], sc[a][:]), (sc["I_bf"][:], sc[b][:]),
                             (sc[aT][:], sc[b][:])],
                        [M[a], M[b], M[aT], px_free[0]])
                p.wait("vector", *g)
                p.do("vector", lambda e, dst=dst, px=px: e.tensor_copy(sc[dst][:], px), "v")
                M[dst] = ("v", p.mark("v"))
                px_free[0] = M[dst]
            return f

        def pr2():
            def f():
                g = mmg(pg, [(sc["I_bf"][:], sc["F1T"][:]), (sc["P2"][:], sc["F1T"][:])],
                        [M["F1T"], M["P2"], pg_free[0]])
                M["A2T"] = EV(sc["A2T"][:], pg, g)
                pg_free[0] = M["A2T"]
                g = mmg(pg, [(sc["I_bf"][:], sc["A2T"][:]), (sc["MT"][:], sc["A2T"][:])],
                        [M["A2T"], M["MT"], pg_free[0]])
                M["P1"] = EV(sc["P1"][:], pg, g)
                pg_free[0] = M["P1"]
            return f

        def pr(pairs):
            def f():
                for dst, pw, x in pairs:
                    g = mmg(pg, [(sc["I_bf"][:], sc[x][:]), (sc[pw][:], sc[x][:])],
                            [M.get(x, M["F1T"]), M.get(pw), pg_free[0]])
                    M[dst] = EV(sc[dst][:], pg, g)
                    pg_free[0] = M[dst]
            return f

        def p4a():
            last_t = (None, 0)
            for dst, col0, srct in [(sc["Vtok"][:], 0, v0), (sc["Vtok"][:], 128, v1),
                                    (B["Ktok"], 0, k0), (B["Ktok"], 128, k1)]:
                if last_t[0] is not None:
                    p.wait("tensor", *last_t)
                if dst is B["Ktok"] and col0 == 0:
                    p.wait("tensor", *core_done[0])
                p.do("tensor", lambda e, srct=srct: e.transpose(sc["ps_t"][:], srct[:, c], sc["I_bf"][:]), "mm")
                gm = p.mark("mm")
                p.wait("scalar", "mm", gm)
                p.do("scalar", lambda e, dst=dst, col0=col0: e.copy(dst[:, col0:col0 + 128], sc["ps_t"][:]), "a")
                last_t = ("a", p.mark("a"))
            M["Ktok"] = last_t

        def p4b():
            p.wait("vector", *M["Ktok"])
            p.do("vector", lambda e: e.tensor_scalar(
                sc["KBtok"][:], B["Ktok"], fe["beta_tok_f"][:, blk:blk + 1], None, ALU.mult), "v")
            M["KBtok"] = ("v", p.mark("v"))
            if fs is not None:
                emit_stats_tile(p, fs, sc["Vtok"][:], 4, blk)
                p.wait("gpsimd", "v")
                p.wait("gpsimd", "a")
                if vtok_sp_box[0] is not None:
                    p.wait_tok("gpsimd", vtok_sp_box[0])
                vtok_sp_box[0] = p.dma("gpsimd", t["v_hbm"][blk * 128:(blk + 1) * 128, :], sc["Vtok"][:])
                p.wait_tok("vector", vtok_sp_box[0])
                p.wait("vector", "a")
            p.do("vector", lambda e: e.tensor_scalar(
                sc["Vtok"][:], sc["Vtok"][:], fe["beta_tok_f"][:, blk:blk + 1], None, ALU.mult), "v")
            M["Vb"] = ("v", p.mark("v"))

        def p5():
            cd = core_done[0]
            MTm = M["P1"]
            px = sc["ps_x"][:]
            g = mmg(px, [(sc["P1"][:], sc["Vtok"][:])], [MTm, M["Vb"], px_free[0]])
            gw1 = mmg(pg, [(sc["KBtok"][:, 128:256], sc["P1"][:])],
                      [M["KBtok"], pg_free[0]])
            M["Utok"] = EV(B["Utok"], px, g, pre_waits=[cd])
            px_free[0] = M["Utok"]
            M["Wdk1"] = EV(B["Wdk1"], sc["ps_g"][:], gw1)
            pg_free[0] = M["Wdk1"]
            g = mmg(sc["ps_x"][:, 0:128], [(sc["KBtok"][:, 0:128], sc["P1"][:])],
                    [px_free[0]])
            M["Wdk0"] = EV(B["Wdk0"], sc["ps_x"][:, 0:128], g)
            px_free[0] = M["Wdk0"]
            g = mmg(pg, [(k0[:, c], B["Wdk0"]), (k1[:, c], B["Wdk1"])],
                    [M["Wdk0"], M["Wdk1"], pg_free[0]])
            gq = mmg(sc["ps_x"][:, 0:128], [(k0[:, c], q0[:, c]), (k1[:, c], q1[:, c])],
                     [px_free[0]])
            M["Lt"] = EV(B["Lt"], pg, g, sc["mLt"][:], pre_waits=[cd])
            pg_free[0] = M["Lt"]
            M["Gt"] = EV(B["Gt"], sc["ps_x"][:, 0:128], gq, sc["mGt"][:], pre_waits=[cd])
            px_free[0] = M["Gt"]

        return [p1,
                sq([[("A2", "AT", "A"), ("A2T", "A", "AT")],
                    [("A4", "A2T", "A2"), ("A4T", "A2", "A2T")]]),
                p4a,
                sq([[("A8", "A4T", "A4"), ("A8T", "A4", "A4T")],
                    [("A16", "A8T", "A8")]]),
                qgrp("P2", "A2", "A4", "A2T"),
                qgrp("MT", "A8", "A16", "A8T"),
                pr2(), p4b, p5]

    def core_chunks(blk):
        c = slice(blk * 128, (blk + 1) * 128)
        B = hb[blk % 2]
        M = H[blk % 2]
        strip_m = {}

        def strip_ap(i):
            return sc["ps_u"][32 * i:32 * (i + 1), :] if i < 3 else sc["ps_o"][0:32, :]

        def solve(i0b, i1b):
            def f():
                for i in range(i0b, i1b):
                    dstp = strip_ap(i)
                    deps = [M["Wdk0"], M["Wdk1"], s0_mark[0]]
                    ops = [(B["Wdk0"][:, 32 * i:32 * (i + 1)], sc["S0"][0][:]),
                           (B["Wdk1"][:, 32 * i:32 * (i + 1)], sc["S0"][1][:])]
                    if i > 0:
                        deps += [M["Lt"], strip_m[i - 1]]
                        ops.append((B["Lt"][0:32 * i, 32 * i:32 * (i + 1)],
                                    sc["u_adj"][0:32 * i, :]))
                    if i == 3:
                        deps.append(o_evac[0])
                    g = mmg(dstp, ops, deps)
                    p.wait("vector", *g)
                    p.wait("vector", *M["Utok"])
                    p.do("vector", lambda e, i=i, dstp=dstp: e.scalar_tensor_tensor(
                        sc["u_adj"][32 * i:32 * (i + 1), :], dstp, -1.0,
                        B["Utok"][32 * i:32 * (i + 1), :], ALU.mult, ALU.add), "v")
                    strip_m[i] = ("v", p.mark("v"))
            return f

        def c3():
            g = mmg(sc["ps_o"][:], [(q0[:, c], sc["S0"][0][:]), (q1[:, c], sc["S0"][1][:]),
                                    (B["Gt"], sc["u_adj"][:])],
                    [s0_mark[0], M["Gt"], strip_m[3]])
            p.wait("scalar", *g)
            if dst_tok_box[0] is not None:
                p.wait_tok("scalar", dst_tok_box[0])
            p.do("scalar", lambda e: e.copy(sc["dtok"][:], sc["ps_o"][:]), "a")
            dm = ("a", p.mark("a"))
            o_evac[0] = dm
            p.wait("gpsimd", *dm)
            dst_tok_box[0] = p.dma("gpsimd", t["delta_hbm"][blk * 128:(blk + 1) * 128, :], sc["dtok"][:])
            if fs is not None:
                p.wait("vector", *dm)
                emit_stats_tile(p, fs, sc["dtok"][:], 3, blk)

        def c4():
            g0 = mmg(sc["ps_S"][0][:], [(B["Ktok"][:, 0:128], sc["u_adj"][:])],
                     [M["Ktok"], strip_m[3], core_done[0]], sem="mm")
            g1 = mmg(sc["ps_S"][1][:], [(B["Ktok"][:, 128:256], sc["u_adj"][:])], [], sem="mm")
            p.wait("vector", *g1)
            p.do("vector", lambda e: e.scalar_tensor_tensor(
                sc["Sf"][0][:], sc["ps_S"][0][:], 1.0, sc["Sf"][0][:], ALU.mult, ALU.add))
            p.do("vector", lambda e: e.scalar_tensor_tensor(
                sc["Sf"][1][:], sc["ps_S"][1][:], 1.0, sc["Sf"][1][:], ALU.mult, ALU.add), "v")
            p.wait("scalar", "v")
            p.do("scalar", lambda e: e.copy(sc["S0"][0][:], sc["Sf"][0][:]))
            p.do("scalar", lambda e: e.copy(sc["S0"][1][:], sc["Sf"][1][:]), "a")
            s0_mark[0] = ("a", p.mark("a"))
            core_done[0] = ("v", p.mark("v"))

        return [solve(0, 2), solve(2, 4), c3, c4]

    for ch in prep_chunks(0):
        ch()
    for blk in range(nblk):
        nxt = prep_chunks(blk + 1) if blk + 1 < nblk else []
        cc = core_chunks(blk)
        order = []
        n = max(len(nxt), len(cc))
        for j in range(n):
            if j < len(nxt):
                order.append(nxt[j])
            if j < len(cc):
                order.append(cc[j])
        for ch in order:
            ch()


def stage_scan_consts():
    i32 = np.arange(128)
    r = i32[:, None]
    ccol = i32[None, :]
    mA = ((r // 32 == ccol // 32) & (ccol < r)).astype(np.float32)
    mAT = mA.T.copy()
    mLt = ((r // 32) < (ccol // 32)).astype(np.float32)
    mGt = (r <= ccol).astype(np.float32)
    I_bf = np.eye(128, dtype=BFNP)
    return {"mA": mA, "mAT": mAT, "mLt": mLt, "mGt": mGt, "I_bf": I_bf}


FIRK = {"s": 3, "l": 31, "wd": 64}
STAT_ROW = {"s": 0, "l": 1, "wd": 2, "d": 3, "v": 4}


def fir_sbuf(nc, stack, fe, sc):
    ent = lambda cm: stack.enter_context(cm)
    sb = {}
    sb["facc"] = [fe["qkv"][0], fe["qkv"][1]]
    sb["firw"] = ent(nc.sbuf_tensor("firw", [128, 2 * 98], F32))
    sb["stats"] = ent(nc.sbuf_tensor("stats_sb", [128, 15 * 64], F32))
    sb["ttile"] = ent(nc.sbuf_tensor("fir_ttile", [128, 256], BF16))
    sb["sscr"] = ent(nc.sbuf_tensor("fir_sscr", [128, 256], F32))
    sb["ps_t"] = sc["ps_t"]
    sb["I_bf"] = sc["I_bf"]
    return sb


def stats_cols(sb, tensor_idx, stat_idx):
    base = (tensor_idx * 3 + stat_idx) * 64
    return sb["stats"][:, base:base + 64]


def emit_stats_tile(p, fs, tok_tile, tensor_idx, blk, sem_engine="vector"):
    st = fs["stats"]
    b0 = tensor_idx * 3 * 64
    p.do("vector", lambda e: e.reduce_sum(
        st[:, b0 + blk:b0 + blk + 1], tok_tile, axis=AX.X))
    p.do("vector", lambda e: e.reduce_max(
        st[:, b0 + 128 + blk:b0 + 128 + blk + 1], tok_tile, axis=AX.X,
        apply_absolute_value=True), "v")
    p.wait("scalar", "v")
    p.do("scalar", lambda e: e.activation(
        fs["sscr"][:], tok_tile, AF.Square,
        accum_out=st[:, b0 + 64 + blk:b0 + 64 + blk + 1]), "a")


def emit_fir_stats(p, t, fe, fs):
    tok_fw = p.dma("gpsimd", fs["firw"][:], t["fir_r"])
    p.wait_tok("vector", tok_fw)
    p.wait_tok("scalar", tok_fw)
    p.wait("vector", "mm")
    p.wait("vector", "a")
    p.wait("scalar", "mm")
    p.wait("scalar", "v")
    v0, v1 = fe["qkv"][4], fe["qkv"][5]
    prod = [fe["qkv"][2], fe["qkv"][3]]
    pmark = {}
    for ti, nm in enumerate(("s", "l", "wd")):
        Kf = FIRK[nm]
        off = {"s": 0, "l": 3, "wd": 34}[nm]
        out_hbm = t[nm + "_hbm"]
        for half in range(2):
            src = (v0, v1)[half]
            acc = fs["facc"][half]
            wcol = lambda j: fs["firw"][:, half * 98 + off + j:half * 98 + off + j + 1]
            p.do("vector", lambda e, src=src, acc=acc, wc=wcol(Kf - 1): e.tensor_scalar(
                acc[:], src[:], wc, None, ALU.mult))
            napc = 0
            for j in range(Kf - 2, -1, -1):
                sh = Kf - 1 - j
                if j % 5 == 2:
                    p.do("vector", lambda e, src=src, acc=acc, wc=wcol(j), sh=sh: e.scalar_tensor_tensor(
                        acc[:, sh:L], src[:, 0:L - sh], wc, acc[:, sh:L], ALU.mult, ALU.add),
                        sem="v" if j == 0 else None)
                else:
                    pb = prod[napc % 2]
                    pm = pmark.get(napc % 2)
                    if pm is not None:
                        p.wait("scalar", "v", pm)
                    p.do("scalar", lambda e, src=src, pb=pb, wc=wcol(j), sh=sh: e.activation(
                        pb[:, sh:L], src[:, 0:L - sh], AF.Identity, scale=wc), "a")
                    p.wait("vector", "a")
                    p.do("vector", lambda e, acc=acc, pb=pb, sh=sh: e.tensor_tensor(
                        acc[:, sh:L], acc[:, sh:L], pb[:, sh:L], ALU.add),
                        sem="v")
                    pmark[napc % 2] = p.mark("v")
                    napc += 1
        p.wait("tensor", "v")
        sp_tok = None
        for blk in range(NBLK):
            c = slice(blk * 128, (blk + 1) * 128)
            for half in range(2):
                p.wait("tensor", "v")
                p.do("tensor", lambda e, half=half, c=c: e.transpose(
                    fs["ps_t"][:], fs["facc"][half][:, c], fs["I_bf"][:]), "mm")
                p.wait("vector", "mm")
                if half == 0 and sp_tok is not None:
                    p.wait_tok("vector", sp_tok)
                if half == 0:
                    p.wait("vector", "a")
                p.do("vector", lambda e, half=half: e.tensor_copy(
                    fs["ttile"][:, half * 128:(half + 1) * 128], fs["ps_t"][:]), "v")
            emit_stats_tile(p, fs, fs["ttile"][:], ti, blk)
            p.wait("gpsimd", "v")
            sp_tok = p.dma("gpsimd", out_hbm[blk * 128:(blk + 1) * 128, :], fs["ttile"][:])
        p.wait_tok("vector", sp_tok)


def emit_stats_finalize(p, t, fs):
    st = fs["stats"]
    p.wait("vector", "a")
    p.wait("vector", "v")
    p.wait("vector", "g")
    p.wait("scalar", "v")
    for ti in range(5):
        b0 = ti * 3 * 64
        p.do("vector", lambda e, b0=b0: e.tensor_scalar_mul(
            st[:, b0:b0 + 64], st[:, b0:b0 + 64], 1.0 / 256.0))
        p.do("vector", lambda e, b0=b0: e.tensor_scalar(
            st[:, b0 + 64:b0 + 128], st[:, b0 + 64:b0 + 128], 1.0 / 256.0, 1e-8,
            ALU.mult, ALU.max), "v")
        p.wait("scalar", "v")
        p.do("scalar", lambda e, b0=b0: e.activation(
            st[:, b0 + 64:b0 + 128], st[:, b0 + 64:b0 + 128], AF.Sqrt), "a")
    p.wait("gpsimd", "a")
    p.wait("gpsimd", "v")
    return p.dma("gpsimd", t["stats_out"], fs["stats"][:])


def stage_fir_weights(fir_s, fir_l, fir_wd, h):
    s = fir_s[h]
    l = fir_l[h]
    w = fir_wd[h]
    cat = np.concatenate([s, l, w], axis=1).astype(np.float32)
    out = np.zeros((128, 196), np.float32)
    out[:, 0:98] = cat[0:128]
    out[:, 98:196] = cat[128:256]
    return out


def emit_ln_tile(p, sb, src, dst, buf):
    p.do("vector", lambda e: e.reduce_sum(sb["ln_m"][buf][:], src, axis=AX.X), "v")
    p.wait("scalar", "v")
    p.do("scalar", lambda e: e.activation(sb["ln_scr"][:], src, AF.Square, accum_out=sb["ln_sq"][buf][:]), "a")
    p.wait("vector", "a")
    p.do("vector", lambda e: e.tensor_scalar_mul(sb["ln_m"][buf][:], sb["ln_m"][buf][:], 1.0 / 1024.0))
    p.do("vector", lambda e: e.tensor_tensor(sb["ln_m2"][buf][:], sb["ln_m"][buf][:], sb["ln_m"][buf][:], ALU.mult))
    p.do("vector", lambda e: e.scalar_tensor_tensor(sb["ln_v"][buf][:], sb["ln_sq"][buf][:], 1.0 / 1024.0, sb["ln_m2"][buf][:], ALU.mult, ALU.subtract), "v")
    p.wait("scalar", "v")
    p.do("scalar", lambda e: e.activation(sb["ln_sd"][buf][:], sb["ln_v"][buf][:], AF.Sqrt, bias=sb["c_eps_ln"][:]), "a")
    p.wait("vector", "a")
    p.do("vector", lambda e: e.reciprocal(sb["ln_r"][buf][:], sb["ln_sd"][buf][:]))
    p.do("vector", lambda e: e.scalar_tensor_tensor(sb["ln_b"][buf][:], sb["ln_m"][buf][:], -1.0, sb["ln_r"][buf][:], ALU.mult, ALU.mult), "v")
    p.wait("scalar", "v")
    p.do("scalar", lambda e: e.activation(dst, src, AF.Identity, bias=sb["ln_b"][buf][:], scale=sb["ln_r"][buf][:]), "a")


TL2 = 2048


def l2_sbuf(nc, stack):
    ent = lambda cm: stack.enter_context(cm)
    sb = {}
    for nm in ["ln_m", "ln_sq", "ln_m2", "ln_v", "ln_sd", "ln_r", "ln_b"]:
        sb[nm] = [ent(nc.sbuf_tensor("l2" + nm + str(i), [128, 1], F32)) for i in range(2)]
    sb["ln_scr"] = ent(nc.sbuf_tensor("l2ln_scr", [128, 1024], F32))
    sb["c_eps_ln"] = ent(nc.sbuf_tensor("l2c_eps_ln", [128, 1], F32))
    sb["hs_t"] = [ent(nc.sbuf_tensor(f"l2hst{i}", [128, 1024], F32)) for i in range(2)]
    sb["xln_t"] = [ent(nc.sbuf_tensor(f"l2xlnt{i}", [128, 1024], BF16)) for i in range(2)]
    sb["x_fm"] = [ent(nc.sbuf_tensor(f"l2xfm{i}", [128, TL2], BF16)) for i in range(8)]
    sb["g1m"] = [ent(nc.sbuf_tensor(f"g1m{i}", [128, 2048], BF16)) for i in range(8)]
    sb["g1s"] = ent(nc.sbuf_tensor("sb_g1s", [64, 2048], BF16))
    sb["stats_fm"] = ent(nc.sbuf_tensor("sb_stats_fm", [64, TL2], BF16))
    sb["g2w"] = ent(nc.sbuf_tensor("g2w", [128, 16 * 20], BF16))
    sb["hdn"] = [ent(nc.sbuf_tensor(f"hdn{i}", [128, 512], BF16)) for i in range(2)]
    sb["lg"] = ent(nc.sbuf_tensor("lg", [20, TL2], F32))
    sb["Ex"] = ent(nc.sbuf_tensor("Ex", [20, TL2], F32))
    sb["rec5"] = ent(nc.sbuf_tensor("rec5", [4, 512], F32))
    sb["w5"] = ent(nc.sbuf_tensor("w5", [20, TL2], BF16))
    sb["w5tok"] = ent(nc.sbuf_tensor("w5tok", [128, 16, 20], BF16))
    sb["w5tokf"] = ent(nc.sbuf_tensor("w5tokf", [128, 16, 20], F32))
    sb["comb"] = [ent(nc.sbuf_tensor(f"comb{i}", [128, 5, 256], BF16)) for i in range(2)]
    sb["o_n"] = [ent(nc.sbuf_tensor("o_n", [128, 1024], BF16)),
                 ent(nc.sbuf_tensor("o_n2", [128, 1024], BF16))]
    sb["o_fm"] = ent(nc.sbuf_tensor("o_fm", [128, 8, 128], BF16))
    sb["owT"] = [ent(nc.sbuf_tensor(f"owT{i}", [128, 1024], BF16)) for i in range(8)]
    sb["x2_t"] = [ent(nc.sbuf_tensor("x2_t", [128, 1024], F32)),
                  ent(nc.sbuf_tensor("x2_t2", [128, 1024], F32))]
    sb["ffn_t"] = ent(nc.sbuf_tensor("ffn_t", [128, 1024], BF16))
    sb["ffn_fm"] = ent(nc.sbuf_tensor("ffn_fm", [128, 8, 128], BF16))
    sb["wr"] = ent(nc.sbuf_tensor("sb_wr", [128, 8 * 8], BF16))
    sb["r_sc"] = ent(nc.sbuf_tensor("r_sc", [8, TL2], F32))
    sb["o_rs"] = ent(nc.sbuf_tensor("o_rs", [128, 1], F32))
    sb["o_ms"] = ent(nc.sbuf_tensor("o_ms", [128, 1], F32))
    sb["onorm"] = ent(nc.sbuf_tensor("sb_onorm", [128, 1024], F32))
    sb["bias20"] = ent(nc.sbuf_tensor("bias20", [20, 1], F32))
    sb["temp20"] = ent(nc.sbuf_tensor("temp20", [20, 1], F32))
    sb["sc20"] = ent(nc.sbuf_tensor("sc20", [20, 1], F32))
    sb["ad20"] = ent(nc.sbuf_tensor("ad20", [20, 1], F32))
    sb["v4"] = ent(nc.sbuf_tensor("v4", [4, 4], F32))
    sb["br8"] = ent(nc.sbuf_tensor("br8", [8, 1], F32))
    sb["ind45"] = ent(nc.sbuf_tensor("sb_ind45", [4, 20], F32))
    sb["ind5a"] = ent(nc.sbuf_tensor("sb_ind5a", [4, 20], F32))
    sb["ind5b"] = ent(nc.sbuf_tensor("sb_ind5b", [4, 20], F32))
    sb["g5"] = ent(nc.sbuf_tensor("sb_g5", [20, 4], F32))
    sb["I20"] = ent(nc.sbuf_tensor("sb_I20", [20, 20], F32))
    sb["I20b"] = ent(nc.sbuf_tensor("sb_I20b", [20, 20], BF16))
    sb["I_bf"] = ent(nc.sbuf_tensor("l2_I_bf", [128, 128], BF16))
    sb["ones8"] = ent(nc.sbuf_tensor("ones8", [8, 1], F32))
    sb["o1x8"] = ent(nc.sbuf_tensor("o1x8", [1, 8], F32))
    sb["o_ms4"] = ent(nc.sbuf_tensor("o_ms4", [128, 4], F32))
    sb["o_rs4"] = ent(nc.sbuf_tensor("o_rs4", [128, 4], F32))
    sb["pG"] = [ent(nc.psum_tensor("pG", [128, 512], F32)),
                ent(nc.psum_tensor("pGb", [128, 512], F32))]
    sb["pO"] = ent(nc.psum_tensor("pO", [128, 512], F32))
    sb["pT"] = [ent(nc.psum_tensor("pT", [128, 128], BF16)),
                ent(nc.psum_tensor("pT2", [128, 128], BF16))]
    sb["pL"] = ent(nc.psum_tensor("pL", [20, 512], F32))
    sb["pX"] = ent(nc.psum_tensor("pX", [20, 512], F32))
    sb["pR"] = ent(nc.psum_tensor("pR", [8, 512], F32))
    sb["pM"] = sb["pX"][0:20, 500:501]
    return sb


def emit_l2(p, t, sb):
    nc = p.nc
    p.do("gpsimd", lambda e: e.memset(sb["c_eps_ln"][:], 1e-5), "g")
    toks = []
    for fb in range(8):
        toks.append(p.dma("gpsimd", sb["g1m"][fb][:], t["g1m"][fb * 128:(fb + 1) * 128, :]))
        toks.append(p.dma("gpsimd", sb["owT"][fb][:], t["owT"][fb * 128:(fb + 1) * 128, :]))
    toks.append(p.dma("gpsimd", sb["g1s"][:], t["g1s"]))
    toks.append(p.dma("gpsimd", sb["stats_fm"][:], t["stats_fm"]))
    toks.append(p.dma("gpsimd", sb["g2w"][:], t["g2wT"]))
    toks.append(p.dma("gpsimd", sb["wr"][:], t["Wr"]))
    toks.append(p.dma("gpsimd", sb["onorm"][:], t["onorm_rep"]))
    toks.append(p.dma("gpsimd", sb["bias20"][:], t["g2_b"].rearrange("(p o) -> p o", o=1)))
    toks.append(p.dma("gpsimd", sb["v4"][:, 0:1], t["value_bias"].rearrange("(p o) -> p o", o=1)))
    toks.append(p.dma("gpsimd", sb["v4"][:, 1:2], t["log_temp"].rearrange("(p o) -> p o", o=1)))
    toks.append(p.dma("gpsimd", sb["v4"][:, 2:3], t["floor_logit"].rearrange("(p o) -> p o", o=1)))
    toks.append(p.dma("gpsimd", sb["br8"][:], t["br"].rearrange("(p o) -> p o", o=1)))
    toks.append(p.dma("gpsimd", sb["ind45"][:], t["ind45"]))
    toks.append(p.dma("gpsimd", sb["ind5a"][:], t["ind5a"]))
    toks.append(p.dma("gpsimd", sb["ind5b"][:], t["ind5b"]))
    toks.append(p.dma("gpsimd", sb["g5"][:], t["g5"]))
    toks.append(p.dma("gpsimd", sb["I20"][:], t["I20"]))
    toks.append(p.dma("gpsimd", sb["I20b"][:], t["I20"]))
    toks.append(p.dma("gpsimd", sb["I_bf"][:], t["I_bf"]))
    p.do("gpsimd", lambda e: e.memset(sb["ones8"][:], 1.0), "g")
    p.do("gpsimd", lambda e: e.memset(sb["o1x8"][:], 1.0), "g")

    apply_marks = {}
    st_toks = {}
    for i in range(16):
        buf = i % 2
        if i >= 2:
            p.wait("sync", "a", apply_marks[i - 2])
        tok_hs = p.dma("sync", sb["hs_t"][buf][:], t["hs"][i * 128:(i + 1) * 128, :])
        p.wait_tok("vector", tok_hs)
        p.wait_tok("scalar", tok_hs)
        p.wait("scalar", "g")
        if i >= 2:
            p.wait_tok("scalar", st_toks[i - 2])
        emit_ln_tile(p, sb, sb["hs_t"][buf][:], sb["xln_t"][buf][:], buf)
        apply_marks[i] = p.mark("a")
        p.wait("gpsimd", "a")
        st_toks[i] = p.dma("gpsimd", t["x_ln_hbm"][i * 128:(i + 1) * 128, :], sb["xln_t"][buf][:])
    for i in range(16):
        p.wait_tok("sync", st_toks[i])
    tr_q = {}
    for tg in range(4):
        for i in range(tg * 4, tg * 4 + 4):
            p.wait_tok("sync", st_toks[i])
        tr_q[tg] = []
        for fb in range(8):
            tr_q[tg].append(p.dma(
                "sync", sb["x_fm"][fb][:, tg * 512:(tg + 1) * 512],
                t["x_ln_hbm"][tg * 512:(tg + 1) * 512, fb * 128:(fb + 1) * 128],
                transpose=True))

    for tk in toks:
        p.wait_tok("tensor", tk)
    for tk in toks:
        p.wait_tok("vector", tk)
        p.wait_tok("scalar", tk)
    p.do("vector", lambda e: e.tensor_scalar_mul(sb["v4"][:, 3:4], sb["v4"][:, 1:2], -1.0), "v")
    p.wait("scalar", "v")
    p.do("scalar", lambda e: e.activation(sb["v4"][:, 1:2], sb["v4"][:, 3:4], AF.Exp), "a")
    p.do("scalar", lambda e: e.activation(sb["v4"][:, 2:3], sb["v4"][:, 2:3], AF.Sigmoid), "a")
    p.wait("vector", "a")
    p.do("vector", lambda e: e.tensor_scalar(sb["v4"][:, 3:4], sb["v4"][:, 2:3], -4.0, 1.0, ALU.mult, ALU.add), "v")
    p.wait("tensor", "v")
    p.wait("tensor", "a")
    p.do("tensor", lambda e: e.matmul(sb["pM"][:], sb["I20"][:], sb["bias20"][:], start=True, stop=False))
    p.do("tensor", lambda e: e.matmul(sb["pM"][:], sb["ind45"][:], sb["v4"][:, 0:1], start=False, stop=True), sem="mm")
    p.wait("vector", "mm")
    p.do("vector", lambda e: e.tensor_copy(sb["bias20"][:], sb["pM"][:]), "v")
    p.wait("tensor", "v")
    p.do("tensor", lambda e: e.matmul(sb["pM"][:], sb["ind5a"][:], sb["v4"][:, 1:2], start=True, stop=True), sem="mm")
    p.wait("vector", "mm")
    p.do("vector", lambda e: e.tensor_copy(sb["temp20"][:], sb["pM"][:]), "v")
    p.wait("tensor", "v")
    p.do("tensor", lambda e: e.matmul(sb["pM"][:], sb["ind5a"][:], sb["v4"][:, 3:4], start=True, stop=True), sem="mm")
    p.wait("vector", "mm")
    p.do("vector", lambda e: e.tensor_copy(sb["sc20"][:], sb["pM"][:]), "v")
    p.wait("tensor", "v")
    p.do("tensor", lambda e: e.matmul(sb["pM"][:], sb["ind5b"][:], sb["v4"][:, 2:3], start=True, stop=True), sem="mm")
    p.wait("vector", "mm")
    p.do("vector", lambda e: e.tensor_copy(sb["ad20"][:], sb["pM"][:]), "v")

    gelu_marks = {}
    for tt in range(TL2 // 512):
        tok0 = tt * 512
        for tk in tr_q[tt]:
            p.wait_tok("tensor", tk)
        for h16 in range(16):
            hb = h16 % 2
            bank = sb["pG"][hb]
            gg = tt * 16 + h16
            if gg >= 2:
                p.wait("tensor", "a", gelu_marks[gg - 2])
            for fb in range(8):
                p.do("tensor", lambda e, fb=fb, h16=h16, tok0=tok0, bank=bank: e.matmul(
                    bank[:], sb["g1m"][fb][:, h16 * 128:(h16 + 1) * 128],
                    sb["x_fm"][fb][:, tok0:tok0 + 512], start=(fb == 0), stop=False))
            p.do("tensor", lambda e, h16=h16, tok0=tok0, bank=bank: e.matmul(
                bank[:], sb["g1s"][:, h16 * 128:(h16 + 1) * 128],
                sb["stats_fm"][:, tok0:tok0 + 512], start=False, stop=True), sem="mm")
            p.wait("scalar", "mm")
            p.do("scalar", lambda e, hb=hb, bank=bank: e.activation(sb["hdn"][hb][:], bank[:], AF.Gelu), "a")
            gelu_marks[gg] = p.mark("a")
            p.wait("tensor", "a", gelu_marks[gg])
            p.do("tensor", lambda e, h16=h16, hb=hb: e.matmul(
                sb["pL"][:], sb["g2w"][:, h16 * 20:(h16 + 1) * 20], sb["hdn"][hb][:],
                start=(h16 == 0), stop=(h16 == 15)), sem="mm" if h16 == 15 else None)
        p.wait("vector", "mm")
        p.do("vector", lambda e, tok0=tok0: e.tensor_scalar(
            sb["lg"][:, tok0:tok0 + 512], sb["pL"][:], sb["bias20"][:], sb["temp20"][:],
            ALU.add, ALU.mult), "v")
        p.wait("scalar", "v")
        p.do("scalar", lambda e, tok0=tok0: e.activation(
            sb["Ex"][:, tok0:tok0 + 512], sb["lg"][:, tok0:tok0 + 512], AF.Exp), "a")
        p.wait("tensor", "a")
        p.wait("tensor", "v")
        p.do("tensor", lambda e, tok0=tok0: e.matmul(
            sb["pX"][0:4, :], sb["g5"][:], sb["Ex"][:, tok0:tok0 + 512],
            start=True, stop=True), sem="mm")
        p.wait("vector", "mm")
        p.do("vector", lambda e: e.reciprocal(sb["rec5"][:], sb["pX"][0:4, :]), "v")
        p.wait("tensor", "v")
        p.do("tensor", lambda e: e.matmul(
            sb["pX"][:], sb["ind5a"][:], sb["rec5"][:],
            start=True, stop=True), sem="mm")
        p.wait("vector", "mm")
        p.do("vector", lambda e, tok0=tok0: e.tensor_tensor(
            sb["w5"][:, tok0:tok0 + 512], sb["Ex"][:, tok0:tok0 + 512], sb["pX"][:], ALU.mult))
        p.do("vector", lambda e, tok0=tok0: e.tensor_scalar(
            sb["w5"][:, tok0:tok0 + 512], sb["w5"][:, tok0:tok0 + 512],
            sb["sc20"][:], sb["ad20"][:], ALU.mult, ALU.add), "v")
        p.wait("tensor", "v")

    for tt16 in range(16):
        p.wait("tensor", "v")
        p.do("tensor", lambda e, tt16=tt16: e.matmul(
            sb["pG"][tt16 % 2][:, 0:20], sb["w5"][:, tt16 * 128:(tt16 + 1) * 128], sb["I20b"][:],
            start=True, stop=True), "mm")
        p.wait("vector", "mm")
        p.do("vector", lambda e, tt16=tt16: e.tensor_copy(
            sb["w5tok"][:, tt16, :], sb["pG"][tt16 % 2][:, 0:20]), "v")
    p.do("vector", lambda e: e.tensor_copy(sb["w5tokf"][:], sb["w5tok"][:]), "v")

    x2st = {}
    prev_fst = None
    fst_toks = {}
    tr_marks = {}
    tr_o_mark = {}
    ln2_mark = {}
    for i in range(16):
        cb = i % 2
        p.wait("sync", "v")
        tok_hs = p.dma("sync", sb["hs_t"][i % 2][:], t["hs"][i * 128:(i + 1) * 128, :])
        if i > 1:
            p.wait("vector", "mm", tr_o_mark[i - 2])
            p.wait_tok("vector", x2st[i - 2])
            p.wait("vector", "a", ln2_mark.get(i - 2, 0))
        for h in range(4):
            hb = h % 2
            p.wait("sync", "v")
            p.wait("sync", "a")
            ctok = p.dma("sync", sb["comb"][hb][:], t["comb"][i, h])
            p.wait_tok("vector", ctok)
            wcol = lambda j: sb["w5tokf"][:, i, 5 * h + j:5 * h + j + 1]
            dst = sb["o_n"][cb][:, h * 256:(h + 1) * 256]
            p.do("vector", lambda e, dst=dst, hb=hb, wc=wcol(0): e.tensor_scalar(
                dst, sb["comb"][hb][:, 0, :], wc, None, ALU.mult))
            for j in range(1, 5):
                p.do("vector", lambda e, dst=dst, hb=hb, j=j, wc=wcol(j): e.scalar_tensor_tensor(
                    dst, sb["comb"][hb][:, j, :], wc, dst, ALU.mult, ALU.add),
                    sem="v" if j == 4 else None)
            p.wait("scalar", "v")
            p.do("scalar", lambda e, dst=dst, h=h: e.activation(
                sb["ln_scr"][:, 0:256], dst, AF.Square,
                accum_out=sb["o_ms4"][:, h:h + 1]), "a")
        p.wait("vector", "a")
        p.do("vector", lambda e: e.tensor_scalar(
            sb["o_ms4"][:], sb["o_ms4"][:], 1.0 / 256.0, 1e-5, ALU.mult, ALU.add), "v")
        p.wait("scalar", "v")
        p.do("scalar", lambda e: e.activation(sb["o_ms4"][:], sb["o_ms4"][:], AF.Sqrt), "a")
        p.wait("vector", "a")
        p.do("vector", lambda e: e.reciprocal(sb["o_rs4"][:], sb["o_ms4"][:]))
        for h in range(4):
            p.do("vector", lambda e, h=h, cb=cb: e.tensor_scalar(
                sb["o_n"][cb][:, h * 256:(h + 1) * 256], sb["o_n"][cb][:, h * 256:(h + 1) * 256],
                sb["o_rs4"][:, h:h + 1], None, ALU.mult))
        p.do("vector", lambda e, cb=cb: e.tensor_tensor(sb["o_n"][cb][:], sb["o_n"][cb][:], sb["onorm"][:], ALU.mult), "v")
        p.wait("tensor", "v")
        p.wait("tensor", "a")
        for fb in range(8):
            pb = sb["pT"][fb % 2]
            if fb >= 2:
                p.wait("tensor", "a", tr_marks[fb - 2])
            p.do("tensor", lambda e, fb=fb, pb=pb, cb=cb: e.transpose(
                pb[:], sb["o_n"][cb][:, fb * 128:(fb + 1) * 128], sb["I_bf"][:]), "mm")
            tr_o_mark[i] = p.mark("mm")
            p.wait("scalar", "mm")
            p.do("scalar", lambda e, fb=fb, pb=pb: e.copy(sb["o_fm"][:, fb, :], pb[:]), "a")
            tr_marks[fb] = p.mark("a")
        p.wait_tok("vector", tok_hs)
        for half in range(2):
            p.wait("tensor", "v")
            p.wait("tensor", "a")
            for fb in range(8):
                p.do("tensor", lambda e, fb=fb, half=half: e.matmul(
                    sb["pO"][:], sb["o_fm"][:, fb, :], sb["owT"][fb][:, half * 512:(half + 1) * 512],
                    start=(fb == 0), stop=(fb == 7)), sem="mm" if fb == 7 else None)
            p.wait("vector", "mm")
            p.do("vector", lambda e, half=half, cb=cb: e.scalar_tensor_tensor(
                sb["x2_t"][cb][:, half * 512:(half + 1) * 512], sb["pO"][:], 1.0,
                sb["hs_t"][cb][:, half * 512:(half + 1) * 512], ALU.mult, ALU.add), "v")
            p.wait("tensor", "v")
        p.wait("gpsimd", "v")
        x2st[i] = p.dma("gpsimd", t["x2_out"][i * 128:(i + 1) * 128, :], sb["x2_t"][cb][:])
        p.wait("scalar", "v")
        if prev_fst is not None:
            p.wait_tok("scalar", prev_fst)
        emit_ln_tile(p, sb, sb["x2_t"][cb][:], sb["ffn_t"][:], i % 2)
        ln2_mark[i] = p.mark("a")
        p.wait("gpsimd", "a")
        prev_fst = p.dma("gpsimd", t["ffn_out"][i * 128:(i + 1) * 128, :], sb["ffn_t"][:])
        fst_toks[i] = prev_fst

    for i in range(16):
        p.wait_tok("sync", fst_toks[i])
    rtoks = []
    for fb in range(8):
        rtoks.append(p.dma("sync", sb["x_fm"][fb][:],
                           t["ffn_out"][:, fb * 128:(fb + 1) * 128], transpose=True))
    for tk in rtoks:
        p.wait_tok("tensor", tk)
    for tg in range(4):
        sl = slice(tg * 512, (tg + 1) * 512)
        for fb in range(8):
            p.do("tensor", lambda e, fb=fb, sl=sl: e.matmul(
                sb["pR"][:], sb["wr"][:, fb * 8:(fb + 1) * 8], sb["x_fm"][fb][:, sl],
                start=(fb == 0), stop=(fb == 7)), sem="mm" if fb == 7 else None)
        p.wait("scalar", "mm")
        p.do("scalar", lambda e, sl=sl: e.activation(
            sb["Ex"][0:8, sl], sb["pR"][:], AF.Exp, bias=sb["br8"][:]), "a")
        p.wait("tensor", "a")
    for tg in range(4):
        sl = slice(tg * 512, (tg + 1) * 512)
        p.wait("vector", "a")
        p.do("tensor", lambda e, sl=sl: e.matmul(
            sb["pX"][0:1, 0:512], sb["ones8"][:], sb["Ex"][0:8, sl],
            start=True, stop=True), sem="mm")
        p.wait("vector", "mm")
        p.do("vector", lambda e, sl=sl: e.reciprocal(sb["rec5"][0:1, 0:512], sb["pX"][0:1, 0:512]), "v")
        p.wait("tensor", "v")
        p.do("tensor", lambda e, sl=sl: e.matmul(
            sb["pX"][0:8, 0:512], sb["o1x8"][:], sb["rec5"][0:1, 0:512],
            start=True, stop=True), sem="mm")
        p.wait("vector", "mm")
        p.do("vector", lambda e, sl=sl: e.tensor_tensor(
            sb["r_sc"][:, sl], sb["Ex"][0:8, sl], sb["pX"][0:8, 0:512], ALU.mult), "v")
        p.wait("tensor", "v")
    p.wait("gpsimd", "v")
    p.dma("gpsimd", t["scores_out"], sb["r_sc"][:])


def stage_l2_consts():
    ind45 = np.zeros((4, 20), np.float32)
    ind5a = np.zeros((4, 20), np.float32)
    ind5b = np.zeros((4, 20), np.float32)
    g5 = np.zeros((20, 4), np.float32)
    for h in range(4):
        ind45[h, 5 * h + 4] = 1
        for j in range(5):
            ind5a[h, 5 * h + j] = 1
            g5[5 * h + j, h] = 1
        for j in range(4):
            ind5b[h, 5 * h + j] = 1
    return {"ind45": ind45, "ind5a": ind5a, "ind5b": ind5b, "g5": g5,
            "I20": np.eye(20, dtype=np.float32),
            "I_bf": np.eye(128, dtype=BFNP)}


NTOK = 16384
KSEL = 2048
BIS16_ROUNDS = 11
NE_H = 1088

L3_SEMS = ["ld", "st", "mm", "v", "a", "g"]


def l3_sbuf(nc, stack):
    ent = lambda cm: stack.enter_context(cm)
    sb = {}
    sb["sc_g"] = ent(nc.sbuf_tensor("sc_gp", [128, 128], F32))
    sb["cmp"] = ent(nc.sbuf_tensor("cmp_scr", [128, 128], F32))
    sb["cnt"] = ent(nc.sbuf_tensor("cnt_col", [128, 1], F32))
    sb["thr"] = ent(nc.sbuf_tensor("thr_col", [128, 1], F32))
    sb["lohi"] = ent(nc.sbuf_tensor("lohi", [1, 8], F32))
    sb["iota15"] = ent(nc.sbuf_tensor("sb_iota15", [1, 15], F32))
    sb["throw"] = ent(nc.sbuf_tensor("throw15", [1, 15], F32))
    sb["cnt15"] = ent(nc.sbuf_tensor("cnt15", [128, 15], F32))
    sb["sel15"] = ent(nc.sbuf_tensor("sel15", [1, 15], F32))
    sb["mask"] = ent(nc.sbuf_tensor("mask_g", [128, 128], BF16))
    sb["pfx"] = ent(nc.sbuf_tensor("pfx_g", [128, 128], BF16))
    sb["pfx2"] = ent(nc.sbuf_tensor("pfx2_g", [128, 128], BF16))
    sb["pm"] = ent(nc.sbuf_tensor("pm_g", [128, 128], BF16))
    sb["ncolf"] = ent(nc.sbuf_tensor("ncolf", [128, 1], F32))
    sb["crow"] = ent(nc.sbuf_tensor("crow", [1, 128], F32))
    sb["pmTf"] = ent(nc.sbuf_tensor("pmTf", [128, 128], F32))
    sb["maskT"] = ent(nc.sbuf_tensor("maskT", [128, 128], F32))
    sb["R"] = ent(nc.sbuf_tensor("R_rank", [128, 128], F32))
    sb["t1"] = ent(nc.sbuf_tensor("l3t1", [128, 128], F32))
    sb["t2"] = ent(nc.sbuf_tensor("l3t2", [128, 128], F32))
    sb["i32a"] = ent(nc.sbuf_tensor("l3i32a", [128, 128], mybir.dt.int32))
    sb["i32b"] = ent(nc.sbuf_tensor("l3i32b", [128, 128], mybir.dt.int32))
    sb["i32c"] = ent(nc.sbuf_tensor("l3i32c", [128, 128], mybir.dt.int32))
    sb["ci1"] = ent(nc.sbuf_tensor("ci1", [128, 128], I16))
    sb["ci2"] = ent(nc.sbuf_tensor("ci2", [128, 128], I16))
    sb["dst1"] = ent(nc.sbuf_tensor("lsd1", [128, NE_H], I16))
    sb["dst2"] = ent(nc.sbuf_tensor("lsd2", [128, NE_H], I16))
    sb["dstf"] = ent(nc.sbuf_tensor("lsdf", [128, NE_H], F32))
    sb["tokc"] = ent(nc.sbuf_tensor("tokc", [1, 2 * NE_H], F32))
    sb["tknS"] = ent(nc.sbuf_tensor("tknS", [16, 136], F32))
    sb["tknw16"] = ent(nc.sbuf_tensor("tknw16", [128, 128], I16))
    sb["q2"] = ent(nc.sbuf_tensor("l3q2", [16, 128], F32))
    sb["gchf"] = ent(nc.sbuf_tensor("gchf", [16, 128], F32))
    sb["gchq"] = ent(nc.sbuf_tensor("gchq", [128, 128], I16))
    sb["gath"] = ent(nc.sbuf_tensor("gath", [128, 16, 64], F32))
    sb["oh"] = ent(nc.sbuf_tensor("ohsel", [128, 64], F32))
    sb["ohf"] = ent(nc.sbuf_tensor("ohself", [128, 64], F32))
    sb["tkO"] = ent(nc.sbuf_tensor("tkO", [128, 16], F32))
    sb["wgO"] = ent(nc.sbuf_tensor("wgO", [128, 16], F32))
    sb["Gsel"] = ent(nc.sbuf_tensor("Gsel", [128, 16], F32))
    sb["tokio"] = ent(nc.sbuf_tensor("sb_tokio", [128, 128], I16))
    sb["trilS"] = ent(nc.sbuf_tensor("sb_trilS", [128, 128], F32))
    sb["idxC"] = ent(nc.sbuf_tensor("sb_idxC", [128, 128], I16))
    sb["cohC"] = ent(nc.sbuf_tensor("sb_cohC", [128, 1024], F32))
    sb["wio64"] = ent(nc.sbuf_tensor("sb_wio64", [128, 64], F32))
    sb["I_bf"] = ent(nc.sbuf_tensor("l3_I_bf", [128, 128], BF16))
    sb["ones_colf"] = ent(nc.sbuf_tensor("l3ones_col", [128, 1], F32))
    sb["ones_row"] = ent(nc.sbuf_tensor("l3ones_row", [1, 128], F32))
    sb["tok4"] = [ent(nc.sbuf_tensor(f"tok4_{i}", [128, 8, 512], BF16)) for i in range(4)]
    sb["wup"] = [ent(nc.sbuf_tensor(f"wup{i}", [128, 2816], BF16)) for i in range(8)]
    sb["wdn"] = [ent(nc.sbuf_tensor(f"wdn{i}", [128, 1024], BF16)) for i in range(22)]
    sb["bup"] = ent(nc.sbuf_tensor("bup", [128, 22], F32))
    sb["bdn"] = ent(nc.sbuf_tensor("sb_bdn", [128, 1024], F32))
    sb["mid"] = [ent(nc.sbuf_tensor(f"mid{i}", [128, 512], BF16)) for i in range(22)]
    sb["orow"] = ent(nc.sbuf_tensor("orow", [128, 1024], F32))
    sb["rows_bf"] = ent(nc.sbuf_tensor("rows_bf", [128, 4, 1024], BF16))
    sb["zrow"] = ent(nc.sbuf_tensor("l3zrow", [128, 1024], BF16))
    sb["pU"] = [ent(nc.psum_tensor("pU0", [128, 512], F32)),
                ent(nc.psum_tensor("pU1", [128, 512], F32))]
    sb["pD"] = [ent(nc.psum_tensor("pD0", [128, 512], F32)),
                ent(nc.psum_tensor("pD1", [128, 512], F32))]
    sb["pS"] = ent(nc.psum_tensor("pS", [1, 512], F32))
    sb["pB"] = ent(nc.psum_tensor("pB", [128, 128], F32))
    sb["pT"] = ent(nc.psum_tensor("l3pT", [128, 128], BF16))
    return sb


def emit_l3(p, t, sb):
    from concourse.library_config import mlp as LIB_MLP, local_scatter as LIB_LS

    p.do("gpsimd", lambda e: e.memset(sb["tknw16"][:], 0), "g")
    p.do("gpsimd", lambda e: e.memset(sb["gchq"][:], 0), "g")
    p.do("gpsimd", lambda e: e.memset(sb["zrow"][:], 0.0), "g")
    p.do("gpsimd", lambda e: e.memset(sb["lohi"][:], 0.0), "g")
    p.do("gpsimd", lambda e: e.memset(sb["thr"][:], 0.0), "g")
    p.do("gpsimd", lambda e: e.memset(sb["ones_colf"][:], 1.0), "g")
    p.do("gpsimd", lambda e: e.memset(sb["ones_row"][:], 1.0), "g")
    toks = []
    for i in range(8):
        toks.append(p.dma("gpsimd", sb["wup"][i][:], t["w_up"][i * 128:(i + 1) * 128, :]))
    for i in range(22):
        toks.append(p.dma("gpsimd", sb["wdn"][i][:], t["w_dn"][i * 128:(i + 1) * 128, :]))
    for nm_s, nm_t in [("sc_g", "scores_g"), ("bup", "bup_r"), ("bdn", "bdn_rep"),
                      ("tokio", "tokio"), ("trilS", "trilS"), ("idxC", "idxC"),
                      ("cohC", "cohC"), ("wio64", "wio64"), ("I_bf", "I_bf"),
                      ("iota15", "iota15")]:
        toks.append(p.dma("gpsimd", sb[nm_s][:], t[nm_t]))
    p.wait("sync", "g")
    ztoks = []
    for r in range(NTOK // 128):
        ztoks.append(p.dma("sync", t["scratch"][r * 128:(r + 1) * 128, :], sb["zrow"][:]))
    for eng in ("vector", "scalar", "tensor"):
        p.wait(eng, "g")
        for tk in toks:
            p.wait_tok(eng, tk)
    p.do("vector", lambda e: e.tensor_scalar(sb["lohi"][0:1, 1:2], sb["lohi"][0:1, 1:2], 0.0, 1.0, ALU.mult, ALU.add), "v")

    for r in range(BIS16_ROUNDS):
        p.do("vector", lambda e: e.tensor_scalar_mul(sb["lohi"][0:1, 1:2], sb["lohi"][0:1, 1:2], 1.0 / 16.0))
        p.do("vector", lambda e: e.tensor_scalar(sb["throw"][:], sb["iota15"][:], sb["lohi"][0:1, 1:2], None, ALU.mult))
        p.do("vector", lambda e: e.tensor_scalar(sb["throw"][:], sb["throw"][:], sb["lohi"][0:1, 0:1], None, ALU.add), "v")
        p.wait("tensor", "v")
        p.do("tensor", lambda e: e.matmul(sb["pB"][:, 0:15], sb["ones_row"][:], sb["throw"][:], start=True, stop=True), sem="mm")
        p.wait("vector", "mm")
        for j in range(15):
            p.do("vector", lambda e, j=j: e.tensor_scalar(
                sb["cmp"][:], sb["sc_g"][:], sb["pB"][:, j:j + 1], 0.0, ALU.is_ge, ALU.add,
                accum_out=sb["cnt15"][:, j:j + 1]), "v" if j == 14 else None)
        p.wait("tensor", "v")
        p.do("tensor", lambda e: e.matmul(sb["pS"][0:1, 0:15], sb["ones_colf"][:], sb["cnt15"][:], start=True, stop=True), sem="mm")
        p.wait("vector", "mm")
        p.do("vector", lambda e: e.tensor_scalar(
            sb["sel15"][:], sb["pS"][0:1, 0:15], float(KSEL), None, ALU.is_ge))
        p.do("vector", lambda e: e.reduce_sum(sb["lohi"][0:1, 2:3], sb["sel15"][:], axis=AX.X))
        p.do("vector", lambda e: e.tensor_scalar(sb["lohi"][0:1, 2:3], sb["lohi"][0:1, 2:3], sb["lohi"][0:1, 1:2], None, ALU.mult))
        p.do("vector", lambda e: e.tensor_tensor(sb["lohi"][0:1, 0:1], sb["lohi"][0:1, 0:1], sb["lohi"][0:1, 2:3], ALU.add), "v")
        p.wait("tensor", "v")
    p.do("tensor", lambda e: e.matmul(sb["pB"][:, 0:1], sb["ones_row"][:], sb["lohi"][0:1, 0:1], start=True, stop=True), sem="mm")
    p.wait("vector", "mm")
    p.do("vector", lambda e: e.tensor_copy(sb["thr"][:], sb["pB"][:, 0:1]), "v")

    p.do("vector", lambda e: e.tensor_scalar(
        sb["mask"][:], sb["sc_g"][:], sb["thr"][:], None, ALU.is_ge), "v")
    p.do("vector", lambda e: e.tensor_copy(sb["pfx"][:], sb["mask"][:]), "v")
    cur, oth = "pfx", "pfx2"
    sh = 1
    while sh < 128:
        p.do("vector", lambda e, cur=cur, oth=oth, sh=sh: e.tensor_copy(sb[oth][:, 0:sh], sb[cur][:, 0:sh]))
        p.do("vector", lambda e, cur=cur, oth=oth, sh=sh: e.tensor_tensor(
            sb[oth][:, sh:128], sb[cur][:, sh:128], sb[cur][:, 0:128 - sh], ALU.add), "v")
        cur, oth = oth, cur
        sh *= 2
    p.do("vector", lambda e, cur=cur: e.tensor_copy(sb["ncolf"][:], sb[cur][:, 127:128]))
    p.do("vector", lambda e, cur=cur: e.tensor_tensor(sb["pm"][:], sb[cur][:], sb["mask"][:], ALU.mult), "v")

    p.wait("tensor", "v")
    p.do("tensor", lambda e: e.matmul(sb["pB"][0:1, 0:128], sb["ncolf"][:], sb["trilS"][:], start=True, stop=True), sem="mm")
    p.wait("vector", "mm")
    p.do("vector", lambda e: e.tensor_copy(sb["crow"][:], sb["pB"][0:1, 0:128]), "v")
    p.wait("tensor", "v")
    p.do("tensor", lambda e: e.transpose(sb["pT"][:], sb["pm"][:], sb["I_bf"][:]), "mm")
    p.wait("vector", "mm")
    p.do("vector", lambda e: e.tensor_copy(sb["pmTf"][:], sb["pT"][:]), "v")
    p.wait("tensor", "v")
    p.do("tensor", lambda e: e.matmul(sb["pB"][:], sb["ones_row"][:], sb["crow"][:], start=True, stop=True), sem="mm")
    p.do("vector", lambda e: e.tensor_scalar(sb["maskT"][:], sb["pmTf"][:], 0.5, None, ALU.is_ge))
    p.wait("vector", "mm")
    p.do("vector", lambda e: e.tensor_tensor(sb["R"][:], sb["pmTf"][:], sb["pB"][:], ALU.add))
    p.do("vector", lambda e: e.tensor_tensor(sb["R"][:], sb["R"][:], sb["maskT"][:], ALU.mult), "v")
    p.wait("tensor", "v")
    p.do("vector", lambda e: e.tensor_tensor(sb["R"][:], sb["R"][:], sb["maskT"][:], ALU.subtract))
    p.do("vector", lambda e: e.tensor_copy(sb["i32a"][:], sb["R"][:]))
    p.do("vector", lambda e: e.tensor_scalar(sb["i32b"][:], sb["i32a"][:], 15, None, ALU.bitwise_and))
    p.do("vector", lambda e: e.tensor_scalar(sb["i32a"][:], sb["i32a"][:], 4, None, ALU.arith_shift_right))
    p.do("vector", lambda e: e.tensor_scalar(sb["i32c"][:], sb["i32b"][:], 7, None, ALU.logical_shift_left))
    p.do("vector", lambda e: e.tensor_scalar(sb["i32b"][:], sb["i32b"][:], 3, None, ALU.logical_shift_left))
    p.do("vector", lambda e: e.tensor_tensor(sb["i32c"][:], sb["i32c"][:], sb["i32b"][:], ALU.add))
    p.do("vector", lambda e: e.tensor_tensor(sb["i32c"][:], sb["i32c"][:], sb["i32a"][:], ALU.add))
    p.do("vector", lambda e: e.tensor_copy(sb["R"][:], sb["i32c"][:]))
    p.do("vector", lambda e: e.tensor_scalar(sb["R"][:], sb["R"][:], 1.0, None, ALU.add))
    p.do("vector", lambda e: e.tensor_tensor(sb["R"][:], sb["R"][:], sb["maskT"][:], ALU.mult))
    p.do("vector", lambda e: e.tensor_scalar(sb["R"][:], sb["R"][:], 1.0, None, ALU.subtract))
    p.do("vector", lambda e: e.tensor_scalar(sb["t1"][:], sb["R"][:], float(NE_H), None, ALU.is_ge))
    p.do("vector", lambda e: e.tensor_scalar(sb["t2"][:], sb["R"][:], float(NE_H - 1), None, ALU.subtract))
    p.do("vector", lambda e: e.tensor_tensor(sb["t2"][:], sb["t2"][:], sb["t1"][:], ALU.mult))
    p.do("vector", lambda e: e.tensor_scalar(sb["t2"][:], sb["t2"][:], 1.0, None, ALU.subtract))
    p.do("vector", lambda e: e.tensor_copy(sb["ci2"][:], sb["t2"][:]))
    p.do("vector", lambda e: e.tensor_scalar(sb["t2"][:], sb["R"][:], 1.0, None, ALU.add))
    p.do("vector", lambda e: e.tensor_tensor(sb["t2"][:], sb["t2"][:], sb["t1"][:], ALU.mult))
    p.do("vector", lambda e: e.tensor_tensor(sb["t2"][:], sb["R"][:], sb["t2"][:], ALU.subtract))
    p.do("vector", lambda e: e.tensor_copy(sb["ci1"][:], sb["t2"][:]), "v")

    p.wait("gpsimd", "v")
    p.do("gpsimd", lambda e: e.load_library(LIB_LS))
    p.do("gpsimd", lambda e: e.local_scatter(
        sb["dst1"][:], sb["tokio"][:], sb["ci1"][:], channels=128,
        num_elems=NE_H, num_idxs=128), "g")
    p.do("gpsimd", lambda e: e.local_scatter(
        sb["dst2"][:], sb["tokio"][:], sb["ci2"][:], channels=128,
        num_elems=NE_H, num_idxs=128), "g")
    p.do("gpsimd", lambda e: e.load_library(LIB_MLP))
    for half, dstn in ((0, "dst1"), (1, "dst2")):
        p.wait("vector", "g")
        p.do("vector", lambda e, dstn=dstn: e.tensor_copy(sb["dstf"][:], sb[dstn][:]), "v")
        off0 = half * NE_H
        for j, (off, w) in enumerate(((0, 512), (512, 512), (1024, 64))):
            p.wait("tensor", "v")
            p.do("tensor", lambda e, off=off, w=w: e.matmul(
                sb["pS"][0:1, 0:w], sb["ones_colf"][:], sb["dstf"][:, off:off + w],
                start=True, stop=True), sem="mm")
            p.wait("vector", "mm")
            p.do("vector", lambda e, off=off, w=w, off0=off0: e.tensor_copy(
                sb["tokc"][0:1, off0 + off:off0 + off + w], sb["pS"][0:1, 0:w]), "v")
            p.wait("tensor", "v")

    p.wait("gpsimd", "v")
    tk_row = p.dma("gpsimd", t["tokrow_hbm"], sb["tokc"][0:1, :])
    p.wait_tok("gpsimd", tk_row)
    t_s1 = p.dma("gpsimd", sb["tknS"][:], t["tokrow_hbm"].rearrange("(q r) -> q r", r=136))
    p.wait_tok("vector", t_s1)
    p.do("vector", lambda e: e.tensor_copy(sb["tknw16"][0:16, 0:128], sb["tknS"][:, 0:128]))
    p.do("vector", lambda e: e.tensor_copy(sb["i32a"][0:16, :], sb["tknS"][:, 0:128]))
    p.do("vector", lambda e: e.tensor_scalar(sb["i32a"][0:16, :], sb["i32a"][0:16, :], 6, None, ALU.arith_shift_right))
    p.do("vector", lambda e: e.tensor_copy(sb["gchq"][0:16, :], sb["i32a"][0:16, :]), "v")
    p.wait("gpsimd", "v")
    t_r1 = p.dma("gpsimd", sb["tknw16"][16:32, :], sb["tknw16"][0:16, :])
    t_r2 = p.dma("gpsimd", sb["gchq"][16:32, :], sb["gchq"][0:16, :])

    p.wait_tok("gpsimd", t_r1)
    ga_toks = []
    for q in range(4):
        gt = p.dma("gpsimd", None, None, fn=lambda e, q=q: e.dma_gather(
            sb["tok4"][q][:], t["ffn_full"], sb["tknw16"][:, q * 32:(q + 1) * 32],
            num_idxs=512, num_idxs_reg=512, elem_size=1024, transpose=True))
        p.wait_tok("gpsimd", gt)
        ga_toks.append(gt)

    p.wait_tok("gpsimd", tk_row)
    for h in range(2):
        gt = p.dma("gpsimd", None, None, fn=lambda e, h=h: e.dma_gather(
            sb["gath"][:, h * 8:(h + 1) * 8, :],
            t["tokrow_hbm"].rearrange("(r x) -> r x", x=64),
            sb["idxC"][:, h * 64:(h + 1) * 64],
            num_idxs=1024, num_idxs_reg=1024, elem_size=64))
        p.wait_tok("gpsimd", gt)
        p.wait_tok("vector", gt)
    for c3 in range(16):
        op = None if c3 < 15 else "v"
        p.do("vector", lambda e, c3=c3: e.tensor_tensor(
            sb["ohf"][:], sb["gath"][:, c3, :], sb["cohC"][:, c3 * 64:(c3 + 1) * 64], ALU.mult))
        p.do("vector", lambda e, c3=c3: e.reduce_sum(
            sb["tkO"][:, c3:c3 + 1], sb["ohf"][:], axis=AX.X), op)
    p.do("vector", lambda e: e.tensor_copy(sb["i32a"][:, 0:16], sb["tkO"][:]))
    p.do("vector", lambda e: e.tensor_scalar(sb["i32a"][:, 0:16], sb["i32a"][:, 0:16], 63, None, ALU.bitwise_and))
    p.do("vector", lambda e: e.tensor_copy(sb["wgO"][:], sb["i32a"][:, 0:16]), "v")

    p.wait("gpsimd", "v")
    p.wait_tok("gpsimd", t_r2)
    for h in range(2):
        gt = p.dma("gpsimd", None, None, fn=lambda e, h=h: e.dma_gather(
            sb["gath"][:, h * 8:(h + 1) * 8, :],
            t["scores_flat"].rearrange("(r x) -> r x", x=64),
            sb["gchq"][:, h * 64:(h + 1) * 64],
            num_idxs=1024, num_idxs_reg=1024, elem_size=64))
        p.wait_tok("gpsimd", gt)
        p.wait_tok("vector", gt)
    for c3 in range(16):
        op = None if c3 < 15 else "v"
        p.do("vector", lambda e, c3=c3: e.tensor_scalar(
            sb["oh"][:], sb["wio64"][:], sb["wgO"][:, c3:c3 + 1], None, ALU.is_equal))
        p.do("vector", lambda e, c3=c3: e.tensor_tensor(
            sb["ohf"][:], sb["gath"][:, c3, :], sb["oh"][:], ALU.mult))
        p.do("vector", lambda e, c3=c3: e.reduce_sum(
            sb["Gsel"][:, c3:c3 + 1], sb["ohf"][:], axis=AX.X), op)
    gsel_mark = p.mark("v")

    prev_scat = None
    for tk in ztoks[-Prog.DMA_POOL:]:
        p.wait_tok("gpsimd", tk)
    silu_mark = {}
    evac_mark = {}
    for sb4 in range(4):
        p.wait_tok("tensor", ga_toks[sb4])
        p.wait_tok("scalar", ga_toks[sb4])
        for mt in range(22):
            bank = sb["pU"][mt % 2]
            gmt = sb4 * 22 + mt
            if gmt >= 2:
                p.wait("tensor", "a", silu_mark[gmt - 2])
            for fb in range(8):
                p.do("tensor", lambda e, mt=mt, fb=fb, sb4=sb4, bank=bank: e.matmul(
                    bank[:], sb["wup"][fb][:, mt * 128:(mt + 1) * 128],
                    sb["tok4"][sb4][:, fb, :],
                    start=(fb == 0), stop=(fb == 7)), sem="mm" if fb == 7 else None)
            p.wait("scalar", "mm")
            p.do("scalar", lambda e, mt=mt, bank=bank: e.activation(
                sb["mid"][mt][:], bank[:], AF.Silu, bias=sb["bup"][:, mt:mt + 1]), "a")
            silu_mark[gmt] = p.mark("a")
        p.wait("tensor", "a")
        p.wait("tensor", "v", gsel_mark)
        p.wait_tok("vector", prev_scat)
        for st in range(4):
            gcol = sb4 * 4 + st
            for oh2 in range(2):
                gidx = st * 2 + oh2
                bank = sb["pD"][gidx % 2]
                if gidx >= 2 or sb4 > 0:
                    p.wait("tensor", "v", evac_mark.get((sb4, gidx - 2),
                                                       evac_mark.get((sb4 - 1, gidx + 6), 0)))
                for mt in range(22):
                    p.do("tensor", lambda e, mt=mt, st=st, oh2=oh2, bank=bank: e.matmul(
                        bank[:], sb["mid"][mt][:, st * 128:(st + 1) * 128],
                        sb["wdn"][mt][:, oh2 * 512:(oh2 + 1) * 512],
                        start=(mt == 0), stop=(mt == 21)), sem="mm" if mt == 21 else None)
                p.wait("vector", "mm")
                p.do("vector", lambda e, oh2=oh2, bank=bank: e.scalar_tensor_tensor(
                    sb["orow"][:, oh2 * 512:(oh2 + 1) * 512], bank[:], 1.0,
                    sb["bdn"][:, oh2 * 512:(oh2 + 1) * 512], ALU.mult, ALU.add))
                p.do("vector", lambda e, oh2=oh2, st=st, gcol=gcol: e.tensor_scalar(
                    sb["rows_bf"][:, st, oh2 * 512:(oh2 + 1) * 512],
                    sb["orow"][:, oh2 * 512:(oh2 + 1) * 512],
                    sb["Gsel"][:, gcol:gcol + 1], None, ALU.mult), "v")
                evac_mark[(sb4, gidx)] = p.mark("v")
        p.wait("gpsimd", "v")
        prev_scat = p.dma("gpsimd", None, None, fn=lambda e, sb4=sb4: e.dma_scatter_add(
            t["scratch"], sb["rows_bf"][:], sb["tknw16"][:, sb4 * 32:(sb4 + 1) * 32],
            num_idxs=512, num_idxs_reg=512, elem_size=1024))
        p.wait_tok("gpsimd", prev_scat)
    p.wait_tok("vector", prev_scat)


def stage_l3_consts():
    i128 = np.arange(128)
    out = {}
    out["iota15"] = np.arange(1, 16, dtype=np.float32)[None, :]
    out["trilS"] = (i128[:, None] < i128[None, :]).astype(np.float32)
    out["tokio"] = (i128[None, :] * 128 + i128[:, None]).astype(np.int16)
    out["wio64"] = np.tile(np.arange(64, dtype=np.float32)[None, :], (128, 1))
    out["I_bf"] = np.eye(128, dtype=BFNP)
    idx = np.zeros((128, 128), np.int16)
    coh = np.zeros((128, 16, 64), np.float32)
    for s in range(KSEL):
        c = (s % 16) * 136 + s // 16
        idx[s % 16, s // 16] = c // 64
        p3, c3 = s % 128, s // 128
        coh[p3, c3, c % 64] = 1.0
    idx[16:32] = idx[0:16]
    out["idxC"] = idx
    out["cohC"] = coh.reshape(128, 1024)
    return out


def stage_l3_inputs(ins, scores_all, e):
    m = {}
    m["scores_g"] = np.ascontiguousarray(scores_all[e].reshape(128, 128))
    m["scores_flat"] = np.ascontiguousarray(scores_all[e])
    g = e // 2
    wup = ins["L0_ffn_W_up"][g].astype(np.float32)
    wupp = np.zeros((1024, 2816), np.float32)
    wupp[:, :2730] = wup
    m["w_up"] = wupp
    wdn = ins["L0_ffn_W_down"][e].astype(np.float32)
    wdnp = np.zeros((2816, 1024), np.float32)
    wdnp[:2730] = wdn
    m["w_dn"] = wdnp
    bup = np.zeros((2816,), np.float32)
    bup[:2730] = ins["L0_ffn_b_up"][g].astype(np.float32)
    m["bup_r"] = np.ascontiguousarray(bup.reshape(22, 128).T)
    m["bdn_rep"] = np.tile(ins["L0_ffn_b_down"][e].astype(np.float32), (128, 1))
    m.update(stage_l3_consts())
    return m


def l4_sbuf(nc, stack):
    ent = lambda cm: stack.enter_context(cm)
    sb = {}
    sb["x2t"] = [ent(nc.sbuf_tensor(f"x2t{i}", [128, 1024], F32)) for i in range(2)]
    sb["scrt"] = [[ent(nc.sbuf_tensor(f"scrt{i}_{e}", [128, 1024], F32)) for e in range(8)]
                  for i in range(2)]
    sb["acc"] = [ent(nc.sbuf_tensor(f"acc{i}", [128, 1024], F32)) for i in range(2)]
    return sb


def emit_l4(p, t, sb):
    st_tok = [None, None]
    vmark = {}
    for i in range(16):
        bi = i % 2
        if i >= 2:
            p.wait("sync", "v", vmark[i - 2])
            p.wait("gpsimd", "v", vmark[i - 2])
        toks = [p.dma("sync", sb["x2t"][bi][:], t["x2"][i * 128:(i + 1) * 128, :])]
        for e2 in range(8):
            toks.append(p.dma("gpsimd", sb["scrt"][bi][e2][:],
                              t["scr8"][e2, i * 128:(i + 1) * 128, :]))
        for tk in toks:
            p.wait_tok("vector", tk)
        p.wait_tok("vector", st_tok[bi])
        p.do("vector", lambda e, bi=bi: e.tensor_tensor(
            sb["acc"][bi][:], sb["x2t"][bi][:], sb["scrt"][bi][0][:], ALU.add))
        for e2 in range(1, 8):
            op = "v" if e2 == 7 else None
            p.do("vector", lambda e, bi=bi, e2=e2: e.tensor_tensor(
                sb["acc"][bi][:], sb["acc"][bi][:], sb["scrt"][bi][e2][:], ALU.add), op)
        vmark[i] = p.mark("v")
        p.wait("sync", "v", vmark[i])
        st_tok[bi] = p.dma("sync", t["out"][i * 128:(i + 1) * 128, :], sb["acc"][bi][:])
    p.final_wait("sync")
    p.final_wait("gpsimd")


def build_l1():
    nc = bass.Bass("TRN2", target_bir_lowering=False)
    t = {}
    t["hs"] = nc.dram_tensor("hs", [L, D], F32, kind="ExternalInput").ap()
    t["wqkv_fm"] = nc.dram_tensor("wqkv_fm", [1024, 769], F32, kind="ExternalInput").ap()
    t["convw_r"] = nc.dram_tensor("convw_r", [128, 24], F32, kind="ExternalInput").ap()
    t["fir_r"] = nc.dram_tensor("fir_r", [128, 196], F32, kind="ExternalInput").ap()
    for nm, dt in [("mA", F32), ("mAT", F32), ("mLt", F32), ("mGt", F32), ("I_bf", BF16)]:
        t[nm] = nc.dram_tensor(nm, [128, 128], dt, kind="ExternalInput").ap()
    t["x_ln_hbm"] = nc.dram_tensor("x_ln_hbm", [L, D], BF16, kind="Internal").ap()
    t["beta_hbm"] = nc.dram_tensor("beta_hbm", [L], BF16, kind="Internal").ap()
    for nm in ("delta_hbm", "s_hbm", "l_hbm", "wd_hbm", "v_hbm"):
        t[nm] = nc.dram_tensor(nm, [L, 256], BF16, kind="ExternalOutput").ap()
    t["stats_out"] = nc.dram_tensor("stats_out", [128, 15 * 64], F32, kind="ExternalOutput").ap()

    with ExitStack() as stack:
        ps = psum_pool(nc, stack)
        sb = frontend_sbuf(nc, stack, ps)
        sc = scan_sbuf(nc, stack, ps)
        fs = fir_sbuf(nc, stack, sb, sc)

        def build_prog(p):
            emit_frontend(p, t, sb)
            emit_frontend_tail(p, t, sb)
            p.barrier_from("tensor", ["v", "a", "mm"])
            p.barrier_from("vector", ["v", "a", "mm"])
            emit_scan(p, t, sb, sc, fs=fs)
            emit_fir_stats(p, t, sb, fs)
            p.barrier_from("gpsimd", ["v", "a", "mm"])
            emit_stats_finalize(p, t, fs)
            for eng in ("gpsimd", "sync", "vector", "scalar", "tensor"):
                p.final_wait(eng)

        run_program(nc, build_prog, L1_SEMS)
    return nc


def stage_l1_inputs(ins, c):
    b, h = c // 4, c % 4
    m = stage_frontend_inputs(
        ins["L0_attn_hidden_states"][b], ins["L0_attn_q_w"], ins["L0_attn_k_w"],
        ins["L0_attn_v_w"], ins["L0_attn_b_w"], ins["L0_attn_convq_w"],
        ins["L0_attn_convk_w"], ins["L0_attn_convv_w"], h)
    m.update(stage_scan_consts())
    m["fir_r"] = stage_fir_weights(ins["L0_attn_fir_s"], ins["L0_attn_fir_l"],
                                   ins["L0_attn_fir_wd"], h)
    return m


def build_l2():
    nc = bass.Bass("TRN2", target_bir_lowering=False)
    t = {}
    t["hs"] = nc.dram_tensor("hs", [2048, 1024], F32, kind="ExternalInput").ap()
    t["g1m"] = nc.dram_tensor("g1m", [1024, 2048], F32, kind="ExternalInput").ap()
    t["g1s"] = nc.dram_tensor("g1s", [64, 2048], F32, kind="ExternalInput").ap()
    t["stats_fm"] = nc.dram_tensor("stats_fm", [64, 2048], F32, kind="ExternalInput").ap()
    t["g2wT"] = nc.dram_tensor("g2wT", [128, 320], F32, kind="ExternalInput").ap()
    t["owT"] = nc.dram_tensor("owT", [1024, 1024], F32, kind="ExternalInput").ap()
    t["Wr"] = nc.dram_tensor("Wr", [128, 64], F32, kind="ExternalInput").ap()
    t["comb"] = nc.dram_tensor("comb", [16, 4, 128, 5, 256], BF16, kind="ExternalInput").ap()
    t["onorm_rep"] = nc.dram_tensor("onorm_rep", [128, 1024], F32, kind="ExternalInput").ap()
    for nm, n in [("g2_b", 20), ("value_bias", 4), ("log_temp", 4),
                  ("floor_logit", 4), ("br", 8)]:
        t[nm] = nc.dram_tensor(nm, [n], F32, kind="ExternalInput").ap()
    for nm, shp, dt in [("ind45", [4, 20], F32), ("ind5a", [4, 20], F32),
                        ("ind5b", [4, 20], F32), ("g5", [20, 4], F32),
                        ("I20", [20, 20], F32), ("I_bf", [128, 128], BF16)]:
        t[nm] = nc.dram_tensor(nm, shp, dt, kind="ExternalInput").ap()
    t["x_ln_hbm"] = nc.dram_tensor("x_ln_hbm", [2048, 1024], BF16, kind="Internal").ap()
    t["x2_out"] = nc.dram_tensor("x2_out", [2048, 1024], F32, kind="ExternalOutput").ap()
    t["ffn_out"] = nc.dram_tensor("ffn_out", [2048, 1024], BF16, kind="ExternalOutput").ap()
    t["scores_out"] = nc.dram_tensor("scores_out", [8, 2048], F32, kind="ExternalOutput").ap()

    with ExitStack() as stack:
        sb = l2_sbuf(nc, stack)

        def build_prog(p):
            emit_l2(p, t, sb)
            for eng in ("gpsimd", "sync", "vector", "scalar", "tensor"):
                p.final_wait(eng)

        run_program(nc, build_prog, L1_SEMS)
    return nc


def stage_l2_inputs(ins, l1res, c):
    b = c // 4
    trow = (c % 4) * 2048
    m = {}
    m["hs"] = np.ascontiguousarray(ins["L0_attn_hidden_states"][b][trow:trow + 2048], dtype=np.float32)
    g1 = ins["L0_attn_g1_w"].astype(np.float32)
    m["g1m"] = np.ascontiguousarray(g1[:, :1024].T)
    g1s = np.zeros((64, 2048), np.float32)
    g1s[:60] = g1[:, 1024:].T
    m["g1s"] = g1s
    sf = np.zeros((64, 2048), np.float32)
    for h in range(4):
        stg = l1res[4 * b + h]["stats_out"].reshape(128, 15, 64)
        for i5 in range(5):
            for s3 in range(3):
                vals = stg[:, i5 * 3 + s3, :].T.reshape(-1)
                sf[i5 * 12 + s3 * 4 + h] = vals[trow:trow + 2048]
    m["stats_fm"] = sf
    m["g2wT"] = np.ascontiguousarray(ins["L0_attn_g2_w"].astype(np.float32).T.reshape(16, 128, 20).transpose(1, 0, 2).reshape(128, 320))
    m["owT"] = np.ascontiguousarray(ins["L0_attn_o_w"].astype(np.float32).T)
    m["Wr"] = np.ascontiguousarray(ins["L0_ffn_Wr"].astype(np.float32).reshape(8, 128, 8).transpose(1, 0, 2).reshape(128, 64))
    comb = np.zeros((16, 4, 128, 5, 256), BFNP)
    for h in range(4):
        r = l1res[4 * b + h]
        for j, key in enumerate(("s_hbm", "l_hbm", "wd_hbm", "delta_hbm", "v_hbm")):
            x = r[key][trow:trow + 2048]
            comb[:, h, :, j, :] = x.reshape(16, 128, 256)
    m["comb"] = comb
    m["onorm_rep"] = np.tile(ins["L0_attn_onorm_w"].astype(np.float32), (128, 4))
    m["g2_b"] = ins["L0_attn_g2_b"].astype(np.float32)
    m["value_bias"] = ins["L0_attn_value_bias"].astype(np.float32)
    m["log_temp"] = ins["L0_attn_log_temp"].astype(np.float32)
    m["floor_logit"] = ins["L0_attn_floor_logit"].astype(np.float32)
    m["br"] = ins["L0_ffn_br"].astype(np.float32)
    m.update(stage_l2_consts())
    return m


def build_l3():
    nc = bass.Bass("TRN2", target_bir_lowering=False)
    t = {}
    t["scores_g"] = nc.dram_tensor("scores_g", [128, 128], F32, kind="ExternalInput").ap()
    t["scores_flat"] = nc.dram_tensor("scores_flat", [NTOK], F32, kind="ExternalInput").ap()
    t["ffn_full"] = nc.dram_tensor("ffn_full", [NTOK, 1024], BF16, kind="ExternalInput").ap()
    t["w_up"] = nc.dram_tensor("w_up", [1024, 2816], F32, kind="ExternalInput").ap()
    t["w_dn"] = nc.dram_tensor("w_dn", [2816, 1024], F32, kind="ExternalInput").ap()
    t["bup_r"] = nc.dram_tensor("bup_r", [128, 22], F32, kind="ExternalInput").ap()
    t["bdn_rep"] = nc.dram_tensor("bdn_rep", [128, 1024], F32, kind="ExternalInput").ap()
    for nm, shape, dt in [("trilS", [128, 128], F32), ("tokio", [128, 128], I16),
                          ("wio64", [128, 64], F32), ("I_bf", [128, 128], BF16),
                          ("idxC", [128, 128], I16), ("cohC", [128, 1024], F32),
                          ("iota15", [1, 15], F32)]:
        t[nm] = nc.dram_tensor(nm, shape, dt, kind="ExternalInput").ap()
    t["tokrow_hbm"] = nc.dram_tensor("tokrow_hbm", [2 * NE_H], F32, kind="Internal").ap()
    t["scratch"] = nc.dram_tensor("scratch", [NTOK, 1024], BF16, kind="ExternalOutput").ap()

    with ExitStack() as stack:
        sb = l3_sbuf(nc, stack)

        def build_prog(p):
            emit_l3(p, t, sb)
            for eng in ("gpsimd", "sync", "vector", "scalar", "tensor"):
                p.final_wait(eng)

        run_program(nc, build_prog, L3_SEMS)
    return nc


def build_l4():
    nc = bass.Bass("TRN2", target_bir_lowering=False)
    t = {}
    t["x2"] = nc.dram_tensor("x2", [2048, 1024], F32, kind="ExternalInput").ap()
    t["scr8"] = nc.dram_tensor("scr8", [8, 2048, 1024], BF16, kind="ExternalInput").ap()
    t["out"] = nc.dram_tensor("out", [2048, 1024], F32, kind="ExternalOutput").ap()
    with ExitStack() as stack:
        sb = l4_sbuf(nc, stack)

        def build_prog(p):
            emit_l4(p, t, sb)
            for eng in ("gpsimd", "sync", "vector", "scalar", "tensor"):
                p.final_wait(eng)

        run_program(nc, build_prog, L3_SEMS)
    return nc


def kernel(**inputs):
    from concourse.bass_utils import run_bass_kernel_spmd
    from concourse.library_overlay import lower_extended_insts
    ins = {k: np.asarray(v) for k, v in inputs.items()}
    cores = list(range(8))

    nc1 = build_l1()
    res1 = run_bass_kernel_spmd(
        nc1, [stage_l1_inputs(ins, c) for c in cores], core_ids=cores).results

    nc2 = build_l2()
    res2 = run_bass_kernel_spmd(
        nc2, [stage_l2_inputs(ins, res1, c) for c in cores], core_ids=cores).results

    ffn_full = np.concatenate([res2[c]["ffn_out"] for c in cores], axis=0)
    scores_all = np.ascontiguousarray(
        np.concatenate([res2[c]["scores_out"] for c in cores], axis=1))

    nc3 = build_l3()
    lower_extended_insts(nc3)
    maps3 = []
    for e in range(8):
        m = stage_l3_inputs(ins, scores_all, e)
        m["ffn_full"] = ffn_full
        maps3.append(m)
    res3 = run_bass_kernel_spmd(nc3, maps3, core_ids=cores).results

    nc4 = build_l4()
    maps4 = []
    for c in cores:
        maps4.append({
            "x2": res2[c]["x2_out"],
            "scr8": np.stack([res3[e]["scratch"][c * 2048:(c + 1) * 2048] for e in range(8)]),
        })
    res4 = run_bass_kernel_spmd(nc4, maps4, core_ids=cores).results

    out = np.concatenate([res4[c]["out"] for c in cores], axis=0)
    return np.ascontiguousarray(out.reshape(2, 8192, 1024))
